# version 04
# baseline (speedup 1.0000x reference)
import numpy as np
import ml_dtypes
from contextlib import ExitStack

import concourse.bass as bass
import concourse.mybir as mybir
from concourse.bass_utils import run_bass_kernel_spmd

F32 = mybir.dt.float32
BF16 = mybir.dt.bfloat16
AF = mybir.ActivationFunctionType
ALU = mybir.AluOpType
AX = mybir.AxisListType

D4 = [0.4829629131445341, 0.8365163037378079, 0.2241438680420134, -0.1294095225512604]
OFFS = [1, 2, 3, 4, 6, 8, 12, 16, 24, 32, 48, 64, 96, 128, 192, 256, 384, 512, 768, 1024, 1536]
EPS = 1e-5
NCORE = 8
SEQ = 2048
DM = 1024
NCH = 8
TT = 512
NTILE = 8
BF16NP = ml_dtypes.bfloat16

_STAGE = "full"
_DBG_TILE = -1
_MAX_OFF = 4096
_NEXP = 8



def _prep_host(inp):
    f32 = np.float32
    c = np.ascontiguousarray

    shared = {}
    shared["wqkv_t"] = c(np.asarray(inp["L0_attn_W_qkv"]).T)
    shared["bqkv_pc"] = c(np.asarray(inp["L0_attn_b_qkv"]).reshape(24, 128).T)
    shared["wgate_t"] = c(np.asarray(inp["L0_attn_W_gate"]).T)
    shared["bgate_pc"] = c(np.asarray(inp["L0_attn_b_gate"]).reshape(8, 128).T)
    shared["wout_t"] = c(np.asarray(inp["L0_attn_W_out"]).T)
    shared["bout_pc"] = c(np.asarray(inp["L0_attn_b_out"]).reshape(8, 128).T)

    wq = np.asarray(inp["L0_attn_W_qscale"])
    wq22 = np.zeros((128, 22), f32)
    wq22[0:64, 0:11] = wq.T
    wq22[64:128, 11:22] = wq.T
    shared["wq22"] = wq22
    prior = np.asarray(inp["L0_attn_scale_gain"])
    shared["prior_row"] = c(prior.T.reshape(1, 176))
    shared["bypass16"] = c(np.asarray(inp["L0_attn_identity_bypass"]).reshape(16, 1))

    shared["ln1w_pc"] = c(np.asarray(inp["L0_ffn_ln1_weight"]).reshape(8, 128).T)
    shared["ln1b_pc"] = c(np.asarray(inp["L0_ffn_ln1_bias"]).reshape(8, 128).T)
    shared["ln2w_pc"] = c(np.asarray(inp["L0_ffn_ln2_weight"]).reshape(8, 128).T)
    shared["ln2b_pc"] = c(np.asarray(inp["L0_ffn_ln2_bias"]).reshape(8, 128).T)
    qkvw = np.asarray(inp["L0_ffn_qkv_weight"])
    rows = []
    for h in range(16):
        kap, j = h // 4, h % 4
        rows.append(qkvw[kap * 384 + j * 64: kap * 384 + (j + 1) * 64])
    for kap in range(4):
        rows.append(qkvw[kap * 384 + 256: kap * 384 + 320])
    for kap in range(4):
        rows.append(qkvw[kap * 384 + 320: kap * 384 + 384])
    shared["wqkv2_t"] = c(np.concatenate(rows, 0).T)
    shared["wproj_t"] = c(np.asarray(inp["L0_ffn_proj_weight"]).T)
    shared["wrout"] = c(np.asarray(inp["L0_ffn_router_weight"]))
    shared["wrout_pc"] = c(np.asarray(inp["L0_ffn_router_weight"]).reshape(8, 128, 8).transpose(1, 0, 2))
    shared["moe_w1"] = c(np.asarray(inp["L0_ffn_moe_w1"]))
    shared["moe_w2"] = c(np.asarray(inp["L0_ffn_moe_w2"]))

    shared["ident_bf"] = np.eye(128, dtype=BF16NP)
    shared["ident_f32"] = np.eye(128, dtype=f32)
    shared["ones1_f32"] = np.ones((1, 128), f32)
    sel16 = np.zeros((16, 8 * 128), BF16NP)
    indT = np.zeros((128, 8 * 16), BF16NP)
    for ch in range(8):
        for p in range(128):
            sel16[2 * ch + p // 64, ch * 128 + p] = 1
            indT[p, ch * 16 + 2 * ch + p // 64] = 1
    shared["sel16"] = sel16
    shared["indT"] = indT
    esel = np.zeros((8, 8 * 128), BF16NP)
    for e in range(8):
        esel[e, e * 128:(e + 1) * 128] = 1
    shared["esel"] = esel

    mbig = np.zeros((640, 11 * 128), BF16NP)

    def off_of(v, j):
        if v == 0:
            return 1 << j
        if v == 1:
            return (1 << (j + 1)) if j <= 9 else -1
        return 3 * (1 << j) if j <= 9 else -1

    for v in range(3):
        for h in range(16):
            for j in range(11):
                o = off_of(v, j)
                if o in OFFS:
                    i = OFFS.index(o)
                    mbig[v * 176 + h * 11 + j, (i // 2) * 128 + (i % 2) * 64 + h] = 1
    shared["mbig"] = mbig

    x_full = np.asarray(inp["L0_attn_x"])
    in_maps = []
    for cid in range(NCORE):
        m = dict(shared)
        m["xs"] = c(np.stack([x_full[cid], x_full[15 - cid]]))
        mask = np.zeros((16, 32), f32)
        for slot, sq in ((0, cid), (1, 15 - cid)):
            for t in range(16):
                if t > sq:
                    mask[:, slot * 16 + t] = -1e30
        m["mask_pc"] = mask
        in_maps.append(m)
    return in_maps



class Ctr:
    __slots__ = ("sem", "v", "name")

    def __init__(self, sem, name):
        self.sem = sem
        self.v = 0
        self.name = name

    def inc(self, inst, n=1):
        inst.then_inc(self.sem, n)
        self.v += n
        return self.v


class Prog:
    def __init__(self, nc, block, ctrs, banks):
        self.nc = nc
        self.block = block
        self.c = ctrs
        self.banks = banks
        self.bank_guard = [None] * len(banks)
        self.last_wait = {}

    def wait(self, eng, ename, ctr, val):
        if val <= 0:
            return
        k = (ename, ctr.name)
        if self.last_wait.get(k, -1) >= val:
            return
        eng.wait_ge(ctr.sem, val)
        self.last_wait[k] = val

    def wait_bank(self, eng, ename, bi):
        g = self.bank_guard[bi]
        if g is not None:
            self.wait(eng, ename, g[0], g[1])

    def set_bank_guard(self, bi, ctr, val):
        self.bank_guard[bi] = (ctr, val)

    def vflush(self, vector):
        return vector.drain()

    def aflush(self, scalar):
        return scalar.drain()

    def vspace(self, vector):
        vector.drain()



def _emit_phase_a(P, io, sb):
    block = P.block
    C = P.c
    pb = P.banks
    xs, x1s = io["xs"], io["x1s"]

    wqkv_guard = {}
    kv_mixer_guard = [0, 0]
    xbuf_free = {}

    for t in range(NTILE):
        si, n0 = t // 4, (t % 4) * TT
        xb = sb["x_tm"][0]
        seq_start = (t % 4 == 0)

        def s0(sync, t=t, si=si, n0=n0, xb=xb):
            if t >= 1:
                P.wait(sync, "sync", C["pe"], xbuf_free[t - 1])
            src = xs[si, n0:n0 + TT, :].rearrange("(tb p) d -> p tb d", p=128)
            C["ld"].inc(sync.dma_start(out=xb[:], in_=src), 16)
        block.sync(s0)
        ld_done = C["ld"].v

        def s1sq(scalar, xb=xb, ld_done=ld_done):
            P.wait(scalar, "scalar", C["ld"], ld_done)
            for tb in range(4):
                scalar.activation(sb["xn_tm"][:, tb, :], xb[:, tb, :], AF.Square)
            C["ac"].inc(P.aflush(scalar))
        block.scalar(s1sq)
        ac_sq = C["ac"].v

        def s1v(vector, xb=xb, ld_done=ld_done, ac_sq=ac_sq):
            P.wait(vector, "vector", C["ld"], ld_done)
            mv = sb["lnmv"]
            for tb in range(4):
                vector.tensor_reduce(mv[:, tb, 0:1], xb[:, tb, :], AX.X, ALU.add)
            P.wait(vector, "vector", C["ac"], ac_sq)
            for tb in range(4):
                vector.tensor_reduce(mv[:, tb, 1:2], sb["xn_tm"][:, tb, :], AX.X, ALU.add)
            P.vspace(vector)
            vector.tensor_scalar(sb["lnmean"][:], mv[:, :, 0], 1.0 / DM, None, ALU.mult)
            P.vspace(vector)
            vector.tensor_tensor(sb["lnsd"][:], sb["lnmean"][:], sb["lnmean"][:], ALU.mult)
            vector.tensor_scalar(sb["lnvar"][:], mv[:, :, 1], 1.0 / DM, None, ALU.mult)
            P.vspace(vector)
            vector.tensor_tensor(sb["lnvar"][:], sb["lnvar"][:], sb["lnsd"][:], ALU.subtract)
            P.vspace(vector)
            vector.tensor_scalar(sb["lnvar"][:], sb["lnvar"][:], EPS, None, ALU.add)
            C["dv"].inc(P.vflush(vector))
        block.vector(s1v)
        dv_var = C["dv"].v

        def s1a(scalar, dv_var=dv_var):
            P.wait(scalar, "scalar", C["dv"], dv_var)
            scalar.activation(sb["lnsd"][:], sb["lnvar"][:], AF.Sqrt)
            C["ac"].inc(P.aflush(scalar))
        block.scalar(s1a)
        ac_sd = C["ac"].v

        def s1v2(vector, xb=xb, ac_sd=ac_sd):
            P.wait(vector, "vector", C["ac"], ac_sd)
            vector.reciprocal(sb["lnrstd"][:], sb["lnsd"][:])
            P.vspace(vector)
            for tb in range(4):
                vector.tensor_scalar(
                    sb["xn_tm"][:, tb, :], xb[:, tb, :],
                    sb["lnmean"][:, tb:tb + 1], sb["lnrstd"][:, tb:tb + 1],
                    ALU.subtract, ALU.mult)
            C["dv"].inc(P.vflush(vector))
        block.vector(s1v2)
        dv_xn = C["dv"].v

        dv_evac_base = C["dv"].v

        def s2t(tensor, dv_xn=dv_xn, dv_evac_base=dv_evac_base):
            P.wait(tensor, "tensor", C["dv"], dv_xn)
            for j in range(32):
                tb, ch = j // 8, j % 8
                bi = j % 2
                if j >= 2:
                    P.wait(tensor, "tensor", C["dv"], dv_evac_base + j - 1)
                else:
                    P.wait_bank(tensor, "tensor", bi)
                pv = pb[bi].ap().bitcast(BF16)
                C["pe"].inc(tensor.transpose(
                    pv[:, 0:128], sb["xn_tm"][:, tb, ch * 128:(ch + 1) * 128],
                    sb["ident_bf"][:]))
        pe_t_base = C["pe"].v
        block.tensor(s2t)

        def s2v(vector, pe_t_base=pe_t_base):
            for j in range(32):
                tb, ch = j // 8, j % 8
                bi = j % 2
                P.wait(vector, "vector", C["pe"], pe_t_base + j + 1)
                pv = pb[bi].ap().bitcast(BF16)
                C["dv"].inc(vector.tensor_copy(
                    sb["xn_fm"][:, ch, tb * 128:(tb + 1) * 128], pv[:, 0:128]))
            C["dv"].inc(P.vflush(vector))
        block.vector(s2v)
        dv_xnfm = C["dv"].v
        P.set_bank_guard(0, C["dv"], dv_xnfm - 1)
        P.set_bank_guard(1, C["dv"], dv_xnfm)

        for oc in range(24):
            gi = t * 40 + oc
            wbuf = sb["wqkv_oc"][gi % 2]

            wsem = C["wq0"] if gi % 2 == 0 else C["wq1"]

            def s3g(g, gi=gi, oc=oc, wbuf=wbuf, wsem=wsem):
                if gi - 2 in wqkv_guard:
                    P.wait(g, "gpsimd", C["pe"], wqkv_guard[gi - 2])
                src = io["wqkv_t"][:, oc * 128:(oc + 1) * 128].rearrange(
                    "(ic p) o -> p ic o", p=128)
                wsem.inc(g.dma_start(out=wbuf[:], in_=src), 16)
            block.gpsimd(s3g)
            gw_done = wsem.v

            bi = 2 + gi % 2

            def s3t(tensor, wbuf=wbuf, bi=bi, gw_done=gw_done, dv_xnfm=dv_xnfm, wsem=wsem):
                P.wait(tensor, "tensor", wsem, gw_done)
                P.wait(tensor, "tensor", C["dv"], dv_xnfm)
                P.wait_bank(tensor, "tensor", bi)
                for ic in range(NCH):
                    inst = tensor.matmul(pb[bi][:], wbuf[:, ic, :], sb["xn_fm"][:, ic, :],
                                         start=(ic == 0), stop=(ic == NCH - 1))
                C["pe"].inc(inst)
            block.tensor(s3t)
            wqkv_guard[gi] = C["pe"].v
            pe_mm = C["pe"].v

            def s3a(scalar, oc=oc, bi=bi, pe_mm=pe_mm, n0=n0, seq_start=seq_start, si=si):
                P.wait(scalar, "scalar", C["pe"], pe_mm)
                if oc < 8:
                    dst = sb["q_fm"][:, oc, :]
                else:
                    if seq_start and si == 1:
                        P.wait(scalar, "scalar", C["dv"], kv_mixer_guard[0])
                    if oc < 16:
                        dst = sb["k_buf"][:, oc - 8, n0:n0 + TT]
                    else:
                        dst = sb["v_buf"][:, oc - 16, n0:n0 + TT]
                inst = scalar.activation(
                    dst, pb[bi][:], AF.Identity, bias=sb["bqkv_pc"][:, oc:oc + 1])
                C["ac"].inc(P.aflush(scalar) if oc == 23 else inst)
            block.scalar(s3a)
            P.set_bank_guard(bi, C["ac"], C["ac"].v)
        ac_qkv = C["ac"].v

        for tb in range(4):
            def s4t(tensor, tb=tb, ac_qkv=ac_qkv):
                P.wait(tensor, "tensor", C["ac"], ac_qkv)
                P.wait_bank(tensor, "tensor", 4)
                for ch in range(NCH):
                    inst = tensor.matmul(pb[4][:, ch * 22:(ch + 1) * 22],
                                         sb["q_fm"][:, ch, tb * 128:(tb + 1) * 128],
                                         sb["wq22_bf"][:], start=True, stop=True)
                C["pe"].inc(inst)
            block.tensor(s4t)
            pe_qo = C["pe"].v

            def s4v(vector, pe_qo=pe_qo):
                P.wait(vector, "vector", C["pe"], pe_qo)
                qo = sb["qo_sb"]
                vector.tensor_tensor(qo[:], pb[4][:, 0:176], sb["prior_bc"][:], ALU.add)
                v176 = qo.rearrange("p (h s) -> p h s", s=11)
                vector.tensor_reduce(sb["qo_m"][:, :, None], v176, AX.X, ALU.max)
                P.vspace(vector)
                mb = sb["qo_m"][:, :, None].broadcast_to([128, 16, 11])
                vector.tensor_tensor(
                    sb["qo_e"].rearrange("p (h s) -> p h s", s=11), v176, mb, ALU.subtract)
                C["dv"].inc(P.vflush(vector))
            block.vector(s4v)
            dv_sub = C["dv"].v
            P.set_bank_guard(4, C["dv"], dv_sub)

            def s4a(scalar, dv_sub=dv_sub):
                P.wait(scalar, "scalar", C["dv"], dv_sub)
                scalar.activation(sb["qo_x"][:], sb["qo_e"][:], AF.Exp)
                C["ac"].inc(P.aflush(scalar))
            block.scalar(s4a)
            ac_exp = C["ac"].v

            def s4v2(vector, tb=tb, ac_exp=ac_exp):
                P.wait(vector, "vector", C["ac"], ac_exp)
                e3 = sb["qo_x"].rearrange("p (h s) -> p h s", s=11)
                vector.tensor_reduce(sb["qo_s"][:, :, None], e3, AX.X, ALU.add)
                P.vspace(vector)
                vector.reciprocal(sb["qo_r"][:], sb["qo_s"][:])
                P.vspace(vector)
                for v in range(3):
                    vector.tensor_scalar(sb["qo_rv"][:, v, :], sb["qo_r"][:],
                                         float(D4[v + 1]), None, ALU.mult)
                P.vspace(vector)
                g = sb["g123_tm"]
                for v in range(3):
                    rb = sb["qo_rv"][:, v, :, None].broadcast_to([128, 16, 11])
                    inst = vector.tensor_tensor(
                        g[:, tb, v * 176:(v + 1) * 176].rearrange("p (h s) -> p h s", s=11),
                        e3, rb, ALU.mult)
                C["dv"].inc(inst)
            block.vector(s4v2)

        def s4fl(vector):
            C["dv"].inc(P.vflush(vector))
        block.vector(s4fl)
        dv_g = C["dv"].v

        dv_evac_base = C["dv"].v

        def s5t(tensor, dv_g=dv_g, dv_evac_base=dv_evac_base):
            P.wait(tensor, "tensor", C["dv"], dv_g)
            for j in range(20):
                tb, rb = j // 5, j % 5
                w = 128 if rb < 4 else 16
                bi = j % 2
                if j >= 2:
                    P.wait(tensor, "tensor", C["dv"], dv_evac_base + j - 1)
                else:
                    P.wait_bank(tensor, "tensor", bi)
                pv = pb[bi].ap().bitcast(BF16)
                C["pe"].inc(tensor.transpose(
                    pv[0:w, 0:128], sb["g123_tm"][:, tb, rb * 128:rb * 128 + w],
                    sb["ident_bf"][:]))
        pe_gt0 = C["pe"].v
        block.tensor(s5t)

        def s5v(vector, pe_gt0=pe_gt0):
            for j in range(20):
                tb, rb = j // 5, j % 5
                w = 128 if rb < 4 else 16
                bi = j % 2
                P.wait(vector, "vector", C["pe"], pe_gt0 + j + 1)
                pv = pb[bi].ap().bitcast(BF16)
                C["dv"].inc(vector.tensor_copy(
                    sb["G_hm"][0:w, rb, tb * 128:(tb + 1) * 128], pv[0:w, 0:128]))
            C["dv"].inc(P.vflush(vector))
        block.vector(s5v)
        dv_ghm = C["dv"].v
        P.set_bank_guard(0, C["dv"], dv_ghm - 1)
        P.set_bank_guard(1, C["dv"], dv_ghm)

        for blk in range(11):
            bi = 4 + blk % 2

            def s6t(tensor, blk=blk, bi=bi, dv_ghm=dv_ghm):
                P.wait(tensor, "tensor", C["dv"], dv_ghm)
                P.wait_bank(tensor, "tensor", bi)
                for rb in range(5):
                    w = 128 if rb < 4 else 16
                    inst = tensor.matmul(pb[bi][:],
                                         sb["mbig_sb"][0:w, rb, blk * 128:(blk + 1) * 128],
                                         sb["G_hm"][0:w, rb, :],
                                         start=(rb == 0), stop=(rb == 4))
                C["pe"].inc(inst)
            block.tensor(s6t)
            pe_cf = C["pe"].v

            def s6v(vector, blk=blk, bi=bi, pe_cf=pe_cf):
                P.wait(vector, "vector", C["pe"], pe_cf)
                C["dv"].inc(vector.tensor_copy(sb["coefS"][:, blk, :], pb[bi][:]))
            block.vector(s6v)
            P.set_bank_guard(bi, C["dv"], C["dv"].v)
        dv_coef = C["dv"].v

        offsets = [0] + [o for o in OFFS if o < n0 + TT and o <= _MAX_OFF]
        for o in offsets:
            lo = max(0, o - n0)
            nv = TT - lo

            def s7p(vector, o=o, lo=lo, nv=nv, n0=n0, ac_qkv=ac_qkv, dv_coef=dv_coef):
                P.wait(vector, "vector", C["ac"], ac_qkv)
                P.wait(vector, "vector", C["dv"], dv_coef)
                vector.tensor_tensor(
                    sb["prod"][:, :, lo:lo + nv],
                    sb["q_fm"][:, :, lo:lo + nv],
                    sb["k_buf"][:, :, n0 + lo - o:n0 + TT - o], ALU.mult)
                C["dv"].inc(P.vflush(vector))
            block.vector(s7p)
            dv_prod = C["dv"].v

            def s7t(tensor, lo=lo, nv=nv, dv_prod=dv_prod):
                P.wait(tensor, "tensor", C["dv"], dv_prod)
                P.wait_bank(tensor, "tensor", 5)
                for ch in range(NCH):
                    inst = tensor.matmul(pb[5][0:16, lo:lo + nv],
                                         sb["indT"][:, ch * 16:(ch + 1) * 16],
                                         sb["prod"][:, ch, lo:lo + nv],
                                         start=(ch == 0), stop=(ch == NCH - 1))
                C["pe"].inc(inst)
            block.tensor(s7t)
            pe_qk = C["pe"].v

            def s7w(vector, o=o, lo=lo, nv=nv, pe_qk=pe_qk):
                P.wait(vector, "vector", C["pe"], pe_qk)
                if o == 0:
                    vector.tensor_scalar(
                        sb["w16_sb"][:, lo:lo + nv], pb[5][0:16, lo:lo + nv],
                        sb["c0_sp"][:], None, ALU.mult)
                else:
                    i = OFFS.index(o)
                    vector.tensor_tensor(
                        sb["w16_sb"][:, lo:lo + nv],
                        sb["coefS"][(i % 2) * 64:(i % 2) * 64 + 16, i // 2, lo:lo + nv],
                        pb[5][0:16, lo:lo + nv], ALU.mult)
                C["dv"].inc(P.vflush(vector))
            block.vector(s7w)
            dv_w16 = C["dv"].v
            P.set_bank_guard(5, C["dv"], dv_w16)

            for ch in range(NCH):
                bi = 6 + ch % 2

                def s7r(tensor, ch=ch, bi=bi, lo=lo, nv=nv, dv_w16=dv_w16):
                    P.wait(tensor, "tensor", C["dv"], dv_w16)
                    P.wait_bank(tensor, "tensor", bi)
                    C["pe"].inc(tensor.matmul(
                        pb[bi][:, lo:lo + nv],
                        sb["sel16"][:, ch * 128:(ch + 1) * 128],
                        sb["w16_sb"][:, lo:lo + nv], start=True, stop=True))
                block.tensor(s7r)
                pe_rep = C["pe"].v

                def s7m(vector, o=o, ch=ch, bi=bi, lo=lo, nv=nv, n0=n0, pe_rep=pe_rep):
                    P.wait(vector, "vector", C["pe"], pe_rep)
                    vsl = sb["v_buf"][:, ch, n0 + lo - o:n0 + TT - o]
                    if o == 0:
                        C["dv"].inc(vector.tensor_tensor(
                            sb["acc"][:, ch, lo:lo + nv], pb[bi][:, lo:lo + nv],
                            vsl, ALU.mult))
                    else:
                        vector.tensor_tensor(
                            sb["mtmp"][ch % 2][:, lo:lo + nv], pb[bi][:, lo:lo + nv],
                            vsl, ALU.mult)
                        C["dv"].inc(vector.tensor_tensor(
                            sb["acc"][:, ch, lo:lo + nv], sb["acc"][:, ch, lo:lo + nv],
                            sb["mtmp"][ch % 2][:, lo:lo + nv], ALU.add))
                block.vector(s7m)
                P.set_bank_guard(bi, C["dv"], C["dv"].v)
        def s7fl(vector):
            C["dv"].inc(P.vflush(vector))
        block.vector(s7fl)
        dv_mixer = C["dv"].v
        if t % 4 == 3:
            kv_mixer_guard[si] = dv_mixer

        for ch in range(NCH):
            gi = t * 40 + 24 + ch
            wbuf = sb["wqkv_oc"][gi % 2]

            wsem = C["wq0"] if gi % 2 == 0 else C["wq1"]

            def s8g(g, gi=gi, ch=ch, wbuf=wbuf, wsem=wsem):
                if gi - 2 in wqkv_guard:
                    P.wait(g, "gpsimd", C["pe"], wqkv_guard[gi - 2])
                src_ = io["wgate_t"][:, ch * 128:(ch + 1) * 128].rearrange(
                    "(ic p) o -> p ic o", p=128)
                wsem.inc(g.dma_start(out=wbuf[:], in_=src_), 16)
            block.gpsimd(s8g)
            gw_g = wsem.v

            bi = 2 + ch % 2

            def s8t(tensor, bi=bi, wbuf=wbuf, gw_g=gw_g, dv_xnfm=dv_xnfm, wsem=wsem):
                P.wait(tensor, "tensor", wsem, gw_g)
                P.wait(tensor, "tensor", C["dv"], dv_xnfm)
                P.wait_bank(tensor, "tensor", bi)
                for ic in range(NCH):
                    inst = tensor.matmul(pb[bi][:], wbuf[:, ic, :],
                                         sb["xn_fm"][:, ic, :],
                                         start=(ic == 0), stop=(ic == NCH - 1))
                C["pe"].inc(inst)
            block.tensor(s8t)
            wqkv_guard[gi] = C["pe"].v
            pe_g = C["pe"].v

            def s8a(scalar, ch=ch, bi=bi, pe_g=pe_g):
                P.wait(scalar, "scalar", C["pe"], pe_g)
                inst = scalar.activation(
                    sb["gate_sb"][:, ch, :], pb[bi][:], AF.Sigmoid,
                    bias=sb["bgate_pc"][:, ch:ch + 1])
                C["ac"].inc(P.aflush(scalar) if ch == 7 else inst)
            block.scalar(s8a)
            P.set_bank_guard(bi, C["ac"], C["ac"].v)
        ac_gate = C["ac"].v

        if t == _DBG_TILE:
            def sdbg(sync, n0=n0, ac_qkv=ac_qkv, dv_g=dv_g, dv_coef=dv_coef,
                     dv_mixer=dv_mixer, ac_gate=ac_gate, dv_xnfm=dv_xnfm):
                P.wait(sync, "sync", C["ac"], ac_qkv)
                C["sp"].inc(sync.dma_start(out=io["dbg_q"], in_=sb["q_fm"][:]), 16)
                C["sp"].inc(sync.dma_start(out=io["dbg_kv"][0], in_=sb["k_buf"][:, :, n0:n0 + TT]), 16)
                C["sp"].inc(sync.dma_start(out=io["dbg_kv"][1], in_=sb["v_buf"][:, :, n0:n0 + TT]), 16)
                P.wait(sync, "sync", C["dv"], dv_xnfm)
                C["sp"].inc(sync.dma_start(out=io["dbg_xn"], in_=sb["xn_fm"][:]), 16)
                P.wait(sync, "sync", C["dv"], dv_g)
                C["sp"].inc(sync.dma_start(out=io["dbg_g"], in_=sb["g123_tm"][:]), 16)
                P.wait(sync, "sync", C["dv"], dv_coef)
                C["sp"].inc(sync.dma_start(out=io["dbg_coef"], in_=sb["coefS"][:]), 16)
                P.wait(sync, "sync", C["dv"], dv_mixer)
                C["sp"].inc(sync.dma_start(out=io["dbg_acc"], in_=sb["acc"][:]), 16)
                P.wait(sync, "sync", C["ac"], ac_gate)
                C["sp"].inc(sync.dma_start(out=io["dbg_gate"], in_=sb["gate_sb"][:]), 16)
            block.sync(sdbg)

        def s9v(vector, ac_gate=ac_gate, dv_mixer=dv_mixer):
            P.wait(vector, "vector", C["ac"], ac_gate)
            P.wait(vector, "vector", C["dv"], dv_mixer)
            vector.tensor_tensor(
                sb["prod"][:], sb["acc"][:], sb["gate_sb"][:], ALU.mult)
            C["dv"].inc(P.vflush(vector))
        block.vector(s9v)
        dv_gg = C["dv"].v

        x1st_guard = [0, 0]
        for oc in range(NCH):
            bi = 2 + oc % 2

            gi = t * 40 + 32 + oc
            wbuf = sb["wqkv_oc"][gi % 2]

            wsem = C["wq0"] if gi % 2 == 0 else C["wq1"]

            def s10g(g, gi=gi, oc=oc, wbuf=wbuf, wsem=wsem):
                if gi - 2 in wqkv_guard:
                    P.wait(g, "gpsimd", C["pe"], wqkv_guard[gi - 2])
                src_ = io["wout_t"][:, oc * 128:(oc + 1) * 128].rearrange(
                    "(ic p) o -> p ic o", p=128)
                wsem.inc(g.dma_start(out=wbuf[:], in_=src_), 16)
            block.gpsimd(s10g)
            gw_y = wsem.v

            def s10t(tensor, bi=bi, wbuf=wbuf, gw_y=gw_y, dv_gg=dv_gg, wsem=wsem):
                P.wait(tensor, "tensor", wsem, gw_y)
                P.wait(tensor, "tensor", C["dv"], dv_gg)
                P.wait_bank(tensor, "tensor", bi)
                for ic in range(NCH):
                    inst = tensor.matmul(pb[bi][:], wbuf[:, ic, :],
                                         sb["prod"][:, ic, :],
                                         start=(ic == 0), stop=(ic == NCH - 1))
                C["pe"].inc(inst)
            block.tensor(s10t)
            wqkv_guard[gi] = C["pe"].v
            pe_y = C["pe"].v

            def s10a(scalar, oc=oc, bi=bi, pe_y=pe_y):
                P.wait(scalar, "scalar", C["pe"], pe_y)
                scalar.activation(
                    sb["y_sb"][:], pb[bi][:], AF.Identity, bias=sb["bout_pc"][:, oc:oc + 1])
                C["ac"].inc(P.aflush(scalar))
            block.scalar(s10a)
            ac_y = C["ac"].v
            P.set_bank_guard(bi, C["ac"], ac_y)

            for tb in range(4):
                bj = tb % 2

                def s10x(tensor, oc=oc, tb=tb, bj=bj, xb=xb):
                    P.wait_bank(tensor, "tensor", bj)
                    C["pe"].inc(tensor.transpose(
                        pb[bj].ap()[:, 0:128], xb[:, tb, oc * 128:(oc + 1) * 128],
                        sb["ident_f32"][:]))
                block.tensor(s10x)
                pe_x = C["pe"].v

                def s10v(vector, oc=oc, tb=tb, bj=bj, pe_x=pe_x, ac_y=ac_y,
                         x1g=x1st_guard[oc % 2]):
                    if x1g != 0:
                        P.wait(vector, "vector", x1g[0], x1g[1])
                    P.wait(vector, "vector", C["pe"], pe_x)
                    P.wait(vector, "vector", C["ac"], ac_y)
                    C["dv"].inc(vector.tensor_tensor(
                        sb["x1st"][oc % 2][:, tb * 128:(tb + 1) * 128],
                        sb["y_sb"][:, tb * 128:(tb + 1) * 128],
                        pb[bj].ap()[:, 0:128], ALU.add))
                block.vector(s10v)
                P.set_bank_guard(bj, C["dv"], C["dv"].v)

            def s10fl(vector):
                C["dv"].inc(P.vflush(vector))
            block.vector(s10fl)
            dv_x1 = C["dv"].v

            xsem = C["sx0"] if oc % 2 == 0 else C["sx1"]

            def s10s(sync, t=t, oc=oc, dv_x1=dv_x1, xsem=xsem):
                P.wait(sync, "sync", C["dv"], dv_x1)
                xsem.inc(sync.dma_start(
                    out=x1s[t, :, oc, :], in_=sb["x1st"][oc % 2][:]), 16)
                if _STAGE == "A":
                    xsem.inc(sync.dma_start(
                        out=io["out_flat"][t, :, oc, :], in_=sb["x1st"][oc % 2][:]), 16)
            block.sync(s10s)
            x1st_guard[oc % 2] = (xsem, xsem.v)

        xbuf_free[t] = C["pe"].v



def _emit_ln_fm(P, sb, src, affine=None, out_bf=None, out_f32=None,
                dv_dep=0, ld_dep=0):
    block = P.block
    C = P.c
    pb = P.banks

    def a_sq(scalar):
        P.wait(scalar, "scalar", C["dv"], dv_dep)
        P.wait(scalar, "scalar", C["ld"], ld_dep)
        for ch in range(NCH):
            scalar.activation(sb["sqscr"][:, ch, :], src[:, ch, :], AF.Square)
        C["ac"].inc(P.aflush(scalar))
    block.scalar(a_sq)
    ac_sq = C["ac"].v

    def t_sum(tensor, ac_sq=ac_sq):
        P.wait(tensor, "tensor", C["ac"], ac_sq)
        P.wait_bank(tensor, "tensor", 4)
        P.wait_bank(tensor, "tensor", 5)
        for ch in range(NCH):
            tensor.matmul(pb[4][0:1, :], sb["ones128c_f32"][:], src[:, ch, :],
                          start=(ch == 0), stop=(ch == NCH - 1))
        for ch in range(NCH):
            inst = tensor.matmul(pb[5][0:1, :], sb["ones128c_bf"][:], sb["sqscr"][:, ch, :],
                                 start=(ch == 0), stop=(ch == NCH - 1))
        C["pe"].inc(inst)
    block.tensor(t_sum)
    pe_sum = C["pe"].v

    def v_rows(vector, pe_sum=pe_sum):
        P.wait(vector, "vector", C["pe"], pe_sum)
        vector.tensor_scalar(sb["rowM"][:], pb[4][0:1, :], 1.0 / DM, None, ALU.mult)
        P.vspace(vector)
        vector.tensor_tensor(sb["rowT"][:], sb["rowM"][:], sb["rowM"][:], ALU.mult)
        vector.tensor_scalar(sb["rowV"][:], pb[5][0:1, :], 1.0 / DM, None, ALU.mult)
        P.vspace(vector)
        vector.tensor_tensor(sb["rowV"][:], sb["rowV"][:], sb["rowT"][:], ALU.subtract)
        P.vspace(vector)
        vector.tensor_scalar(sb["rowV"][:], sb["rowV"][:], EPS, None, ALU.add)
        C["dv"].inc(P.vflush(vector))
    block.vector(v_rows)
    dv_rows = C["dv"].v
    P.set_bank_guard(4, C["dv"], dv_rows)
    P.set_bank_guard(5, C["dv"], dv_rows)

    def a_sd(scalar, dv_rows=dv_rows):
        P.wait(scalar, "scalar", C["dv"], dv_rows)
        scalar.activation(sb["rowSD"][:], sb["rowV"][:], AF.Sqrt)
        C["ac"].inc(P.aflush(scalar))
    block.scalar(a_sd)
    ac_sd = C["ac"].v

    def v_ab(vector, ac_sd=ac_sd):
        P.wait(vector, "vector", C["ac"], ac_sd)
        vector.reciprocal(sb["rowA"][:], sb["rowSD"][:])
        P.vspace(vector)
        vector.tensor_tensor(sb["rowB"][:], sb["rowM"][:], sb["rowA"][:], ALU.mult)
        P.vspace(vector)
        vector.tensor_scalar(sb["rowB"][:], sb["rowB"][:], -1.0, None, ALU.mult)
        C["dv"].inc(P.vflush(vector))
    block.vector(v_ab)
    dv_ab = C["dv"].v

    def t_rep(tensor, dv_ab=dv_ab):
        P.wait(tensor, "tensor", C["dv"], dv_ab)
        P.wait_bank(tensor, "tensor", 4)
        P.wait_bank(tensor, "tensor", 5)
        tensor.matmul(pb[4][:], sb["ones1_f32"][:], sb["rowA"][:], start=True, stop=True)
        inst = tensor.matmul(pb[5][:], sb["ones1_f32"][:], sb["rowB"][:], start=True, stop=True)
        C["pe"].inc(inst)
    block.tensor(t_rep)
    pe_rep = C["pe"].v

    def v_norm(vector, pe_rep=pe_rep, affine=affine, out_bf=out_bf, out_f32=out_f32):
        P.wait(vector, "vector", C["pe"], pe_rep)
        vector.tensor_copy(sb["A_s"][:], pb[4][:])
        vector.tensor_copy(sb["B_s"][:], pb[5][:])
        P.vspace(vector)
        ab = sb["A_s"][:, None, :].broadcast_to([128, NCH, TT])
        bb = sb["B_s"][:, None, :].broadcast_to([128, NCH, TT])
        tgt_plain = out_f32 if out_f32 is not None else (out_bf if affine is None else None)
        if tgt_plain is not None and affine is None:
            vector.tensor_tensor(sb["tmpf"][:], src, ab, ALU.mult)
            vector.tensor_tensor(tgt_plain, sb["tmpf"][:], bb, ALU.add)
        else:
            vector.tensor_tensor(sb["tmpf"][:], src, ab, ALU.mult)
            vector.tensor_tensor(sb["tmpf"][:], sb["tmpf"][:], bb, ALU.add)
            w_pc, b_pc = affine
            for ch in range(NCH):
                vector.tensor_scalar(out_bf[:, ch, :], sb["tmpf"][:, ch, :],
                                     w_pc[:, ch:ch + 1], b_pc[:, ch:ch + 1],
                                     ALU.mult, ALU.add)
        C["dv"].inc(P.vflush(vector))
    block.vector(v_norm)
    P.set_bank_guard(4, C["dv"], C["dv"].v)
    P.set_bank_guard(5, C["dv"], C["dv"].v)
    return C["dv"].v



def _emit_phase_b(P, io, sb):
    block = P.block
    C = P.c
    pb = P.banks
    wb_guard = {}

    def bgate_g(g):
        P.wait(g, "gpsimd", C["dv"], sb["phasea_dv"])
        P.wait(g, "gpsimd", C["ac"], sb["phasea_ac"])
        P.wait(g, "gpsimd", C["pe"], sb["phasea_pe"])
        P.wait(g, "gpsimd", C["sp"], C["sp"].v)
        P.wait(g, "gpsimd", C["sx0"], C["sx0"].v)
        P.wait(g, "gpsimd", C["sx1"], C["sx1"].v)
    block.gpsimd(bgate_g)

    def bgate_s(sync):
        P.wait(sync, "sync", C["dv"], sb["phasea_dv"])
        P.wait(sync, "sync", C["ac"], sb["phasea_ac"])
        P.wait(sync, "sync", C["pe"], sb["phasea_pe"])
    block.sync(bgate_s)

    for t in range(NTILE):
        si, n0 = t // 4, (t % 4) * TT

        def b0(sync, t=t):
            P.wait(sync, "sync", C["sp"], sb["sp_tile_guard"][0])
            P.wait(sync, "sync", C["sx0"], C["sx0"].v)
            P.wait(sync, "sync", C["sx1"], C["sx1"].v)
            src = io["x1s"][t].rearrange("p c n -> p c n")
            C["ld"].inc(sync.dma_start(out=sb["xb1"][:], in_=src), 16)
        block.sync(b0)
        ld_x1 = C["ld"].v

        def b0w(vector, t=t, ld_x1=ld_x1):
            P.wait(vector, "vector", C["ld"], ld_x1)
            P.wait(vector, "vector", C["sp"], sb["sp_tile_guard"][0])
        block.vector(b0w)

        def b0wa(scalar, t=t, ld_x1=ld_x1):
            P.wait(scalar, "scalar", C["ld"], ld_x1)
            P.wait(scalar, "scalar", C["sp"], sb["sp_tile_guard"][0])
        block.scalar(b0wa)

        dv_hid = _emit_ln_fm(P, sb, sb["xb1"][:], out_f32=sb["hidf"][:], ld_dep=ld_x1)
        dv_ln1 = _emit_ln_fm(P, sb, sb["hidf"][:],
                             affine=(sb["ln1w_pc"], sb["ln1b_pc"]), out_bf=sb["lnx_bf"],
                             dv_dep=dv_hid)

        def b1s(sync, t=t, dv_hid=dv_hid):
            P.wait(sync, "sync", C["dv"], dv_hid)
            C["sp"].inc(sync.dma_start(out=io["hidden_s"][t], in_=sb["hidf"][:]), 16)
        block.sync(b1s)

        for oc in range(12):
            gi = 1000 + t * 12 + oc
            wbuf = sb["wb2"][gi % 2]

            wsem = C["wq0"] if gi % 2 == 0 else C["wq1"]

            def b2g(g, gi=gi, oc=oc, wbuf=wbuf, wsem=wsem):
                if gi - 2 in wb_guard:
                    P.wait(g, "gpsimd", C["pe"], wb_guard[gi - 2])
                src_ = io["wqkv2_t"][:, oc * 128:(oc + 1) * 128].rearrange(
                    "(ic p) o -> p ic o", p=128)
                wsem.inc(g.dma_start(out=wbuf[:], in_=src_), 16)
            block.gpsimd(b2g)
            gw_done = wsem.v

            bi = 2 + gi % 2

            def b2t(tensor, wbuf=wbuf, bi=bi, gw_done=gw_done, dv_ln1=dv_ln1, wsem=wsem):
                P.wait(tensor, "tensor", wsem, gw_done)
                P.wait(tensor, "tensor", C["dv"], dv_ln1)
                P.wait_bank(tensor, "tensor", bi)
                for ic in range(NCH):
                    inst = tensor.matmul(pb[bi][:], wbuf[:, ic, :], sb["lnx_bf"][:, ic, :],
                                         start=(ic == 0), stop=(ic == NCH - 1))
                C["pe"].inc(inst)
            block.tensor(b2t)
            wb_guard[gi] = C["pe"].v
            pe_mm = C["pe"].v

            def b2a(scalar, oc=oc, bi=bi, pe_mm=pe_mm):
                P.wait(scalar, "scalar", C["pe"], pe_mm)
                if oc < 8:
                    inst = scalar.activation(sb["q2t"][:, oc, :], pb[bi][:],
                                             AF.Identity, bias=0.0, scale=0.125)
                else:
                    inst = scalar.activation(sb["kvstage"][:, oc - 8, :], pb[bi][:],
                                             AF.Identity)
                C["ac"].inc(P.aflush(scalar) if oc == 11 else inst)
            block.scalar(b2a)
            P.set_bank_guard(bi, C["ac"], C["ac"].v)
        ac_qkv2 = C["ac"].v

        def b3s(sync, t=t, si=si, n0=n0, ac_qkv2=ac_qkv2):
            P.wait(sync, "sync", C["ac"], ac_qkv2)
            C["sp"].inc(sync.dma_start(out=io["q2s"][t], in_=sb["q2t"][:]), 16)
            dst = io["kvc"].rearrange("(g p) c -> p g c", p=128)[:, :, si * SEQ + n0:si * SEQ + n0 + TT]
            C["sp"].inc(sync.dma_start(out=dst, in_=sb["kvstage"][:]), 16)
        block.sync(b3s)
        sb["sp_tile_guard"][0] = C["sp"].v
    sp_b1 = C["sp"].v

    def bcc(g, sp_b1=sp_b1):
        P.wait(g, "gpsimd", C["sp"], sp_b1)
        C["cc"].inc(g.collective_compute(
            "AllGather", ALU.bypass, replica_groups=[list(range(NCORE))],
            ins=[io["kvc"]], outs=[io["gath"]]))
    block.gpsimd(bcc)
    cc_done = C["cc"].v

    for t in range(NTILE):
        si, n0 = t // 4, (t % 4) * TT
        tmax = 8 if si == 0 else 16

        def c0(sync, t=t, cc_done=cc_done):
            P.wait(sync, "sync", C["cc"], cc_done)
            if t > 0:
                P.wait(sync, "sync", C["dv"], sb["b2_reuse_guard"][0])
            P.wait(sync, "sync", C["sp"], sb["sp_tile_guard"][0])
            C["ld"].inc(sync.dma_start(out=sb["q2t"][:], in_=io["q2s"][t]), 16)
            C["ld"].inc(sync.dma_start(out=sb["xb1"][:], in_=io["x1s"][t]), 16)
            C["ld"].inc(sync.dma_start(out=sb["hidf"][:], in_=io["hidden_s"][t]), 16)
        block.sync(c0)
        ld_t = C["ld"].v

        def c0w(vector, ld_t=ld_t):
            P.wait(vector, "vector", C["ld"], ld_t)
        block.vector(c0w)

        def c0wa(scalar, ld_t=ld_t):
            P.wait(scalar, "scalar", C["ld"], ld_t)
        block.scalar(c0wa)

        for tq in range(tmax):
            r = min(tq, 15 - tq)
            hh = 0 if tq == r else 1

            def c1k(sync, t=t, si=si, n0=n0, r=r, hh=hh, tq=tq, ld_t=ld_t):
                if (t, tq) != (0, 0):
                    P.wait(sync, "sync", C["dv"], sb["kdup_guard"][0])
                for kap in range(4):
                    for dup in range(2):
                        srcr = io["gath"][r * 512 + kap * 64: r * 512 + (kap + 1) * 64,
                                          hh * SEQ + n0: hh * SEQ + n0 + TT]
                        C["lk"].inc(sync.dma_start(
                            out=sb["kdup"][dup * 64:(dup + 1) * 64, kap, :], in_=srcr), 16)
            block.sync(c1k)
            ld_k = C["lk"].v

            def c1p(vector, ld_k=ld_k, ld_t=ld_t):
                P.wait(vector, "vector", C["lk"], ld_k)
                P.wait(vector, "vector", C["ld"], ld_t)
                kb = sb["kdup"][:, :, None, :].broadcast_to([128, 4, 2, TT])
                vector.tensor_tensor(
                    sb["prod2"].rearrange("p (a b) n -> p a b n", b=2),
                    sb["q2t"].rearrange("p (a b) n -> p a b n", b=2), kb, ALU.mult)
                C["dv"].inc(P.vflush(vector))
            block.vector(c1p)
            dv_prod = C["dv"].v
            sb["kdup_guard"][0] = dv_prod

            def c1t(tensor, dv_prod=dv_prod):
                P.wait(tensor, "tensor", C["dv"], dv_prod)
                P.wait_bank(tensor, "tensor", 5)
                for ch in range(NCH):
                    inst = tensor.matmul(pb[5][0:16, :],
                                         sb["indT"][:, ch * 16:(ch + 1) * 16],
                                         sb["prod2"][:, ch, :],
                                         start=(ch == 0), stop=(ch == NCH - 1))
                C["pe"].inc(inst)
            block.tensor(c1t)
            pe_sc = C["pe"].v

            def c1a(scalar, si=si, tq=tq, pe_sc=pe_sc):
                P.wait(scalar, "scalar", C["pe"], pe_sc)
                inst = scalar.activation(sb["scores"][0:16, tq, :], pb[5][0:16, :],
                                         AF.Identity,
                                         bias=sb["mask_sb"][:, si * 16 + tq: si * 16 + tq + 1])
                C["ac"].inc(P.aflush(scalar) if tq == tmax - 1 else inst)
            block.scalar(c1a)
            P.set_bank_guard(5, C["ac"], C["ac"].v)
        ac_sc = C["ac"].v

        def c2v(vector, tmax=tmax, ac_sc=ac_sc):
            P.wait(vector, "vector", C["ac"], ac_sc)
            s = sb["scores"]
            vector.tensor_tensor(sb["m16"][0:16, :], s[0:16, 0, :], s[0:16, 1, :], ALU.max)
            for tq in range(2, tmax):
                vector.tensor_tensor(sb["m16"][0:16, :], sb["m16"][0:16, :], s[0:16, tq, :], ALU.max)
            P.vspace(vector)
            mb = sb["m16"][0:16, None, :].broadcast_to([16, tmax, TT])
            vector.tensor_tensor(sb["esc"][0:16, 0:tmax, :], s[0:16, 0:tmax, :], mb, ALU.subtract)
            C["dv"].inc(P.vflush(vector))
        block.vector(c2v)
        dv_sub2 = C["dv"].v

        def c2a(scalar, tmax=tmax, dv_sub2=dv_sub2):
            P.wait(scalar, "scalar", C["dv"], dv_sub2)
            scalar.activation(sb["esc"][0:16, 0:tmax, :], sb["esc"][0:16, 0:tmax, :], AF.Exp)
            C["ac"].inc(P.aflush(scalar))
        block.scalar(c2a)
        ac_exp2 = C["ac"].v

        def c3v(vector, tmax=tmax, ac_exp2=ac_exp2):
            P.wait(vector, "vector", C["ac"], ac_exp2)
            e = sb["esc"]
            vector.tensor_tensor(sb["s16"][0:16, :], e[0:16, 0, :], e[0:16, 1, :], ALU.add)
            for tq in range(2, tmax):
                vector.tensor_tensor(sb["s16"][0:16, :], sb["s16"][0:16, :], e[0:16, tq, :], ALU.add)
            P.vspace(vector)
            vector.reciprocal(sb["r16"][0:16, :], sb["s16"][0:16, :])
            P.vspace(vector)
            rb = sb["r16"][0:16, None, :].broadcast_to([16, tmax, TT])
            vector.tensor_tensor(sb["en"][0:16, 0:tmax, :], e[0:16, 0:tmax, :], rb, ALU.mult)
            C["dv"].inc(P.vflush(vector))
        block.vector(c3v)
        dv_en = C["dv"].v

        for tq in range(tmax):
            r = min(tq, 15 - tq)
            hh = 0 if tq == r else 1

            def c4v(sync, si=si, n0=n0, r=r, hh=hh, tq=tq):
                if (t, tq) != (0, 0):
                    P.wait(sync, "sync", C["dv"], sb["vdup_guard"][0])
                for kap in range(4):
                    for dup in range(2):
                        srcr = io["gath"][r * 512 + 256 + kap * 64: r * 512 + 256 + (kap + 1) * 64,
                                          hh * SEQ + n0: hh * SEQ + n0 + TT]
                        C["lv"].inc(sync.dma_start(
                            out=sb["vdup"][dup * 64:(dup + 1) * 64, kap, :], in_=srcr), 16)
            block.sync(c4v)
            ld_v = C["lv"].v

            for ch in range(NCH):
                bi = 6 + ch % 2

                def c4r(tensor, ch=ch, bi=bi, tq=tq, dv_en=dv_en):
                    P.wait(tensor, "tensor", C["dv"], dv_en)
                    P.wait_bank(tensor, "tensor", bi)
                    C["pe"].inc(tensor.matmul(
                        pb[bi][:], sb["sel16"][:, ch * 128:(ch + 1) * 128],
                        sb["en"][0:16, tq, :], start=True, stop=True))
                block.tensor(c4r)
                pe_rep = C["pe"].v

                def c4m(vector, ch=ch, bi=bi, tq=tq, pe_rep=pe_rep, ld_v=ld_v):
                    P.wait(vector, "vector", C["pe"], pe_rep)
                    P.wait(vector, "vector", C["lv"], ld_v)
                    vsl = sb["vdup"][:, ch // 2, :]
                    if tq == 0:
                        inst = vector.tensor_tensor(
                            sb["attn_acc"][:, ch, :], pb[bi][:], vsl, ALU.mult)
                    else:
                        vector.tensor_tensor(sb["atmp"][ch % 2][:], pb[bi][:], vsl, ALU.mult)
                        inst = vector.tensor_tensor(
                            sb["attn_acc"][:, ch, :], sb["attn_acc"][:, ch, :],
                            sb["atmp"][ch % 2][:], ALU.add)
                    C["dv"].inc(inst)
                block.vector(c4m)
                P.set_bank_guard(bi, C["dv"], C["dv"].v)
            def c4fl(vector):
                C["dv"].inc(P.vflush(vector))
            block.vector(c4fl)
            sb["vdup_guard"][0] = C["dv"].v
        dv_attn = C["dv"].v

        def c5v(vector, dv_attn=dv_attn):
            vector.tensor_copy(sb["prod2"][:], sb["attn_acc"][:])
            C["dv"].inc(P.vflush(vector))
        block.vector(c5v)
        dv_attnbf = C["dv"].v

        for oc in range(NCH):
            gi = 3000 + t * 8 + oc
            wbuf = sb["wb2"][gi % 2]

            wsem = C["wq0"] if gi % 2 == 0 else C["wq1"]

            def c6g(g, gi=gi, oc=oc, wbuf=wbuf, wsem=wsem):
                if gi - 2 in wb_guard:
                    P.wait(g, "gpsimd", C["pe"], wb_guard[gi - 2])
                src_ = io["wproj_t"][:, oc * 128:(oc + 1) * 128].rearrange(
                    "(ic p) o -> p ic o", p=128)
                wsem.inc(g.dma_start(out=wbuf[:], in_=src_), 16)
            block.gpsimd(c6g)
            gw_done = wsem.v

            bi = 2 + gi % 2

            def c6t(tensor, wbuf=wbuf, bi=bi, gw_done=gw_done, dv_attnbf=dv_attnbf, wsem=wsem):
                P.wait(tensor, "tensor", wsem, gw_done)
                P.wait(tensor, "tensor", C["dv"], dv_attnbf)
                P.wait_bank(tensor, "tensor", bi)
                for ic in range(NCH):
                    inst = tensor.matmul(pb[bi][:], wbuf[:, ic, :], sb["prod2"][:, ic, :],
                                         start=(ic == 0), stop=(ic == NCH - 1))
                C["pe"].inc(inst)
            block.tensor(c6t)
            wb_guard[gi] = C["pe"].v
            pe_pr = C["pe"].v

            def c6v(vector, oc=oc, bi=bi, pe_pr=pe_pr):
                P.wait(vector, "vector", C["pe"], pe_pr)
                vector.tensor_tensor(sb["tmpf"][:, oc, :], sb["hidf"][:, oc, :],
                                     pb[bi][:], ALU.add)
                inst = vector.tensor_tensor(sb["hidf"][:, oc, :], sb["xb1"][:, oc, :],
                                            sb["tmpf"][:, oc, :], ALU.add)
                C["dv"].inc(inst)
            block.vector(c6v)
            P.set_bank_guard(bi, C["dv"], C["dv"].v)
        def c6fl(vector):
            C["dv"].inc(P.vflush(vector))
        block.vector(c6fl)
        dv_han = C["dv"].v

        def c7s(sync, t=t, dv_han=dv_han):
            P.wait(sync, "sync", C["dv"], dv_han)
            C["sp"].inc(sync.dma_start(out=io["accs"][t], in_=sb["hidf"][:]), 16)
            if _STAGE == "B":
                C["sp"].inc(sync.dma_start(out=io["out_flat"][t], in_=sb["hidf"][:]), 16)
        block.sync(c7s)

        def c7v(vector, dv_han=dv_han):
            vector.tensor_copy(sb["xb1"][:], sb["tmpf"][:])
            C["dv"].inc(P.vflush(vector))
        block.vector(c7v)
        dv_hancp = C["dv"].v

        dv_ln2 = _emit_ln_fm(P, sb, sb["xb1"][:],
                             affine=(sb["ln2w_pc"], sb["ln2b_pc"]), out_bf=sb["lnx_bf"],
                             dv_dep=dv_hancp)

        def c8s(sync, t=t, dv_ln2=dv_ln2):
            P.wait(sync, "sync", C["dv"], dv_ln2)
            C["sp"].inc(sync.dma_start(out=io["ln2s"][t], in_=sb["lnx_bf"][:]), 16)
        block.sync(c8s)

        def c9t(tensor, dv_ln2=dv_ln2):
            P.wait(tensor, "tensor", C["dv"], dv_ln2)
            P.wait_bank(tensor, "tensor", 4)
            for ic in range(NCH):
                inst = tensor.matmul(pb[4][0:8, :], sb["wrout_bf"][:, ic, :],
                                     sb["lnx_bf"][:, ic, :],
                                     start=(ic == 0), stop=(ic == NCH - 1))
            C["pe"].inc(inst)
        block.tensor(c9t)
        pe_lg = C["pe"].v

        def c9v(vector, pe_lg=pe_lg):
            P.wait(vector, "vector", C["pe"], pe_lg)
            vector.tensor_copy(sb["lg8"][0:8, :], pb[4][0:8, :])
            C["dv"].inc(P.vflush(vector))
        block.vector(c9v)
        dv_lg = C["dv"].v
        P.set_bank_guard(4, C["dv"], dv_lg)

        for tb in range(4):
            def c9tt2(tensor, tb=tb, dv_lg=dv_lg):
                P.wait(tensor, "tensor", C["dv"], dv_lg)
                P.wait_bank(tensor, "tensor", tb % 2)
                C["pe"].inc(tensor.transpose(
                    pb[tb % 2].ap()[0:128, 0:8],
                    sb["lg8"][0:8, tb * 128:(tb + 1) * 128], sb["ident_f32"][0:8, 0:8]))
            block.tensor(c9tt2)
            pe_lt = C["pe"].v

            def c9v2(vector, t=t, tb=tb, pe_lt=pe_lt):
                P.wait(vector, "vector", C["pe"], pe_lt)
                inst = vector.tensor_copy(sb["lgt"][:, tb, :], pb[tb % 2].ap()[0:128, 0:8])
                if _STAGE == "C1":
                    inst = vector.tensor_copy(sb["lgt_dbg"][:, t * 4 + tb, :],
                                              pb[tb % 2].ap()[0:128, 0:8])
                C["dv"].inc(P.vflush(vector) if tb == 3 else inst)
            block.vector(c9v2)
            P.set_bank_guard(tb % 2, C["dv"], C["dv"].v)
        dv_lgt = C["dv"].v

        for tb in range(4):
            def c10v(vector, tb=tb, dv_lgt=dv_lgt):
                P.wait(vector, "vector", C["dv"], dv_lgt)
                lg = sb["lgt"][:, tb, :]
                vector.max(sb["m8"][:], lg)
                P.vspace(vector)
                vector.tensor_scalar(sb["ind8"][:], lg, sb["m8"][:, 1:2], None, ALU.is_ge)
                vector.tensor_scalar(sb["negm"][:], sb["m8"][:, 0:1], -1.0, None, ALU.mult)
                C["dv"].inc(P.vflush(vector))
            block.vector(c10v)
            dv_t2 = C["dv"].v

            def c10a(scalar, tb=tb, dv_t2=dv_t2):
                P.wait(scalar, "scalar", C["dv"], dv_t2)
                scalar.activation(sb["e8"][:], sb["lgt"][:, tb, :], AF.Exp, bias=sb["negm"][:])
                scalar.activation(sb["e2b"][:], sb["m8"][:, 1:2], AF.Exp, bias=sb["negm"][:])
                C["ac"].inc(P.aflush(scalar))
            block.scalar(c10a)
            ac_t2 = C["ac"].v

            def c10v2(vector, t=t, tb=tb, ac_t2=ac_t2):
                P.wait(vector, "vector", C["ac"], ac_t2)
                vector.tensor_scalar(sb["e2b"][:], sb["e2b"][:], 1.0, None, ALU.add)
                P.vspace(vector)
                vector.reciprocal(sb["rden"][:], sb["e2b"][:])
                P.vspace(vector)
                vector.tensor_tensor(sb["w8"][:], sb["e8"][:], sb["ind8"][:], ALU.mult)
                P.vspace(vector)
                vector.tensor_scalar(sb["w8"][:], sb["w8"][:], sb["rden"][:], None, ALU.mult)
                C["dv"].inc(P.vflush(vector))
            block.vector(c10v2)
            dv_w8 = C["dv"].v

            def c10t(tensor, tb=tb, dv_w8=dv_w8):
                P.wait(tensor, "tensor", C["dv"], dv_w8)
                P.wait_bank(tensor, "tensor", tb % 2)
                C["pe"].inc(tensor.transpose(
                    pb[tb % 2].ap()[0:8, 0:128], sb["w8"][:], sb["ident_f32"][:]))
            block.tensor(c10t)
            pe_w = C["pe"].v

            def c10v3(vector, t=t, tb=tb, pe_w=pe_w):
                P.wait(vector, "vector", C["pe"], pe_w)
                inst = vector.tensor_copy(sb["w_em"][0:8, t * TT + tb * 128: t * TT + (tb + 1) * 128],
                                          pb[tb % 2].ap()[0:8, 0:128])
                C["dv"].inc(inst)
            block.vector(c10v3)
            P.set_bank_guard(tb % 2, C["dv"], C["dv"].v)

        def cfl(vector):
            C["dv"].inc(P.vflush(vector))
        block.vector(cfl)
        sb["b2_reuse_guard"][0] = C["dv"].v
        sb["sp_tile_guard"][0] = C["sp"].v



def _emit_phase_c(P, io, sb):
    block = P.block
    C = P.c
    pb = P.banks
    w1_guard = {}
    w2_guard = {}
    ln2_guard = {}

    def cgate_g(g):
        P.wait(g, "gpsimd", C["dv"], sb["phaseb_dv"][0])
        P.wait(g, "gpsimd", C["ac"], sb["phaseb_ac"][0])
        P.wait(g, "gpsimd", C["pe"], sb["phaseb_pe"][0])
        P.wait(g, "gpsimd", C["sp"], sb["phaseb_sp"][0])
    block.gpsimd(cgate_g)

    def cgate_s(sync):
        P.wait(sync, "sync", C["dv"], sb["phaseb_dv"][0])
        P.wait(sync, "sync", C["ac"], sb["phaseb_ac"][0])
        P.wait(sync, "sync", C["pe"], sb["phaseb_pe"][0])
    block.sync(cgate_s)

    for H in range(2):
        def d0(sync, H=H):
            if H == 0:
                P.wait(sync, "sync", C["sp"], sb["phaseb_sp"][0])
            else:
                P.wait(sync, "sync", C["so0"], C["so0"].v)
                P.wait(sync, "sync", C["so1"], C["so1"].v)
            for tt in range(4):
                C["ld"].inc(sync.dma_start(
                    out=sb["acc_h"][:, :, tt * TT:(tt + 1) * TT], in_=io["accs"][H * 4 + tt]), 16)
        block.sync(d0)
        ld_acc = C["ld"].v

        def d0v(vector, ld_acc=ld_acc):
            P.wait(vector, "vector", C["ld"], ld_acc)
        block.vector(d0v)

        for e in range(_NEXP):
            def d1g(g, H=H, e=e):
                key = (H, e)
                prev = (H, e - 1) if e > 0 else (H - 1, _NEXP - 1)
                if prev in w2_guard:
                    P.wait(g, "gpsimd", C["dv"], w2_guard[prev])
                src_ = io["moe_w2"][e].rearrange("(fq fp p) h -> p fq fp h", p=128, fq=8)
                for fq in range(8):
                    C["g2"].inc(g.dma_start(
                        out=sb["w2_sb"][:, fq * 4:(fq + 1) * 4, :], in_=src_[:, fq]), 16)
            block.gpsimd(d1g)
            gw_w2 = C["g2"].v

            for tt in range(4):
                bw = tt % 2

                def d2t(tensor, H=H, e=e, tt=tt, bw=bw):
                    P.wait(tensor, "tensor", C["dv"], sb["phaseb_dv"][0])
                    P.wait_bank(tensor, "tensor", bw)
                    C["pe"].inc(tensor.matmul(
                        pb[bw][:], sb["esel_sb"][0:8, e * 128:(e + 1) * 128],
                        sb["w_em"][0:8, (H * 4 + tt) * TT:(H * 4 + tt + 1) * TT],
                        start=True, stop=True))
                block.tensor(d2t)
                pe_wr = C["pe"].v

                def d2v(vector, tt=tt, bw=bw, pe_wr=pe_wr):
                    P.wait(vector, "vector", C["pe"], pe_wr)
                    inst = vector.tensor_copy(sb["wrep"][:, tt, :], pb[bw][:])
                    C["dv"].inc(P.vflush(vector) if tt == 3 else inst)
                block.vector(d2v)
                P.set_bank_guard(bw, C["dv"], C["dv"].v)
            dv_wr = C["dv"].v

            for tt in range(4):
                t_glob = H * 4 + tt
                def d3s(sync, t_glob=t_glob):
                    hist = sb["ln2_hist"]
                    if hist:
                        P.wait(sync, "sync", C["pe"], hist[-1])
                    C["ld"].inc(sync.dma_start(
                        out=sb["ln2t"][0][:], in_=io["ln2s"][t_glob]), 16)
                block.sync(d3s)
                ld_ln2 = C["ld"].v

                for fcB in range(8):
                    gi = H * 256 + e * 32 + tt * 8 + fcB
                    w1b = sb["w1_sb"][gi % 2]

                    wsem = C["w10"] if gi % 2 == 0 else C["w11"]

                    def d4g(g, H=H, e=e, fcB=fcB, gi=gi, w1b=w1b, wsem=wsem):
                        if gi - 2 in w1_guard:
                            P.wait(g, "gpsimd", C["pe"], w1_guard[gi - 2])
                        src_ = io["moe_w1"][e][:, fcB * 512:(fcB + 1) * 512].rearrange(
                            "(hc p) (fp f) -> p hc fp f", p=128, f=128)
                        wsem.inc(g.dma_start(out=w1b[:], in_=src_), 16)
                    block.gpsimd(d4g)
                    gw_w1 = wsem.v

                    for fp in range(4):
                        bi = 2 + (fcB * 4 + fp) % 2

                        def d5t(tensor, fp=fp, bi=bi, w1b=w1b, gw_w1=gw_w1, ld_ln2=ld_ln2, tt=tt, wsem=wsem):
                            P.wait(tensor, "tensor", wsem, gw_w1)
                            P.wait(tensor, "tensor", C["ld"], ld_ln2)
                            P.wait_bank(tensor, "tensor", bi)
                            for hc in range(NCH):
                                inst = tensor.matmul(pb[bi][:], w1b[:, hc, fp, :],
                                                     sb["ln2t"][0][:, hc, :],
                                                     start=(hc == 0), stop=(hc == NCH - 1))
                            C["pe"].inc(inst)
                        block.tensor(d5t)
                        pe_fc1 = C["pe"].v
                        if fp == 3:
                            w1_guard[gi] = pe_fc1
                            if fcB == 7:
                                sb["ln2_hist"].append(pe_fc1)

                        def d5a(scalar, fcB=fcB, fp=fp, bi=bi, pe_fc1=pe_fc1):
                            P.wait(scalar, "scalar", C["pe"], sb["prev_y_pe"][0])
                            P.wait(scalar, "scalar", C["pe"], pe_fc1)
                            inst = scalar.activation(sb["act_w"][:, fcB * 4 + fp, :],
                                                     pb[bi][:], AF.Gelu)
                            C["ac"].inc(P.aflush(scalar) if fp == 3 else inst)
                        block.scalar(d5a)
                        P.set_bank_guard(bi, C["ac"], C["ac"].v)
                ac_act = C["ac"].v

                for hc in range(NCH):
                    bi = 6 + hc % 2

                    def d7t(tensor, hc=hc, bi=bi, ac_act=ac_act, gw_w2=gw_w2):
                        P.wait(tensor, "tensor", C["g2"], gw_w2)
                        P.wait(tensor, "tensor", C["ac"], ac_act)
                        P.wait_bank(tensor, "tensor", bi)
                        for fc in range(32):
                            inst = tensor.matmul(pb[bi][:], sb["w2_sb"][:, fc, hc * 128:(hc + 1) * 128],
                                                 sb["act_w"][:, fc, :],
                                                 start=(fc == 0), stop=(fc == 31))
                        C["pe"].inc(inst)
                    block.tensor(d7t)
                    pe_y = C["pe"].v

                    def d7v(vector, H=H, e=e, hc=hc, bi=bi, tt=tt, pe_y=pe_y, dv_wr=dv_wr):
                        P.wait(vector, "vector", C["pe"], pe_y)
                        P.wait(vector, "vector", C["dv"], dv_wr)
                        if _STAGE == "C2" and H == 0 and e == 0 and tt == 0 and hc == 0:
                            vector.tensor_copy(sb["ydump"][:], pb[bi][:])
                        vector.tensor_tensor(sb["ctmp"][0][:], pb[bi][:],
                                             sb["wrep"][:, tt, :], ALU.mult)
                        inst = vector.tensor_tensor(
                            sb["acc_h"][:, hc, tt * TT:(tt + 1) * TT],
                            sb["acc_h"][:, hc, tt * TT:(tt + 1) * TT],
                            sb["ctmp"][0][:], ALU.add)
                        C["dv"].inc(inst)
                    block.vector(d7v)
                    P.set_bank_guard(bi, C["dv"], C["dv"].v)
                sb["prev_y_pe"][0] = C["pe"].v
                def d7fl(vector):
                    C["dv"].inc(P.vflush(vector))
                block.vector(d7fl)
                if _STAGE == "C2" and H == 0 and e == 0 and tt == 0:
                    def dmpc2(g):
                        P.wait(g, "gpsimd", C["dv"], C["dv"].v)
                        ofl = io["out"].rearrange("s n d -> (s n d)")
                        C["gw"].inc(g.dma_start(
                            out=ofl[0:128 * 32 * TT].rearrange("(p a n) -> p a n", p=128, a=32),
                            in_=sb["act_w"][:]), 16)
                        C["gw"].inc(g.dma_start(
                            out=ofl[128 * 32 * TT:128 * 32 * TT + 128 * TT]
                            .rearrange("(p n) -> p n", p=128), in_=sb["wrep"][:, 0, :]), 16)
                        C["gw"].inc(g.dma_start(
                            out=ofl[128 * 33 * TT:128 * 33 * TT + 128 * 8 * TT]
                            .rearrange("(p c n) -> p c n", p=128, c=8),
                            in_=sb["acc_h"][:, :, 0:TT]), 16)
                        C["gw"].inc(g.dma_start(
                            out=ofl[128 * 41 * TT:128 * 41 * TT + 128 * 4 * DM]
                            .rearrange("(p c n) -> p c n", p=128, c=4),
                            in_=sb["w2_sb"][:, 0:4, :]), 16)
                        C["gw"].inc(g.dma_start(
                            out=ofl[128 * 49 * TT:128 * 49 * TT + 128 * TT]
                            .rearrange("(p n) -> p n", p=128),
                            in_=sb["ydump"][:]), 16)
                    block.gpsimd(dmpc2)
            w2_guard[(H, e)] = C["dv"].v
        dv_half = C["dv"].v
        if _STAGE == "C2":
            continue

        for tt in range(4):
            for hc in range(NCH):
                for tb2 in range(4):
                    bj = tb2 % 2

                    def d8t(tensor, H=H, tt=tt, hc=hc, tb2=tb2, bj=bj, dv_half=dv_half):
                        P.wait(tensor, "tensor", C["dv"], dv_half)
                        P.wait_bank(tensor, "tensor", bj)
                        C["pe"].inc(tensor.transpose(
                            pb[bj].ap()[:, 0:128],
                            sb["acc_h"][:, hc, tt * TT + tb2 * 128: tt * TT + (tb2 + 1) * 128],
                            sb["ident_f32"][:]))
                    block.tensor(d8t)
                    pe_tr = C["pe"].v

                    def d8v(vector, bj=bj, pe_tr=pe_tr, og=sb["ost_guard"][bj]):
                        P.wait(vector, "vector", C["pe"], pe_tr)
                        if og:
                            P.wait(vector, "vector", og[0], og[1])
                        inst = vector.tensor_copy(sb["ost"][bj][:], pb[bj].ap()[:, 0:128])
                        C["dv"].inc(P.vflush(vector))
                    block.vector(d8v)
                    dv_o = C["dv"].v
                    P.set_bank_guard(bj, C["dv"], dv_o)

                    osem = C["so0"] if bj == 0 else C["so1"]

                    def d8s(sync, H=H, tt=tt, hc=hc, tb2=tb2, bj=bj, dv_o=dv_o, osem=osem):
                        P.wait(sync, "sync", C["dv"], dv_o)
                        dst = io["out"][H, tt * TT + tb2 * 128: tt * TT + (tb2 + 1) * 128,
                                        hc * 128:(hc + 1) * 128]
                        osem.inc(sync.dma_start(out=dst, in_=sb["ost"][bj][:]), 16)
                    block.sync(d8s)
                    sb["ost_guard"][bj] = (osem, osem.v)
        sb["out_guard"][0] = C["sp"].v
        if True:
            pass



def build_program():
    nc = bass.Bass("TRN2", target_bir_lowering=False)

    io = {}
    for nm, shp, dt in [
        ("xs", [2, SEQ, DM], F32),
        ("wqkv_t", [DM, 3072], F32), ("bqkv_pc", [128, 24], F32),
        ("wgate_t", [DM, DM], F32), ("bgate_pc", [128, 8], F32),
        ("wout_t", [DM, DM], F32), ("bout_pc", [128, 8], F32),
        ("wq22", [128, 22], F32), ("prior_row", [1, 176], F32),
        ("bypass16", [16, 1], F32),
        ("ln1w_pc", [128, 8], F32), ("ln1b_pc", [128, 8], F32),
        ("ln2w_pc", [128, 8], F32), ("ln2b_pc", [128, 8], F32),
        ("wqkv2_t", [DM, 1536], F32), ("wproj_t", [DM, DM], F32),
        ("wrout", [DM, 8], F32), ("wrout_pc", [128, 8, 8], F32),
        ("moe_w1", [8, DM, 4096], F32), ("moe_w2", [8, 4096, DM], F32),
        ("ident_bf", [128, 128], BF16), ("ident_f32", [128, 128], F32),
        ("ones1_f32", [1, 128], F32),
        ("sel16", [16, 8 * 128], BF16), ("indT", [128, 8 * 16], BF16),
        ("esel", [8, 8 * 128], BF16), ("mbig", [640, 11 * 128], BF16),
        ("mask_pc", [16, 32], F32),
    ]:
        io[nm] = nc.dram_tensor(nm, shp, dt, kind="ExternalInput").ap()

    out = nc.dram_tensor("out", [2, SEQ, DM], F32, kind="ExternalOutput").ap()
    io["out"] = out
    io["out_flat"] = out.rearrange("s n d -> (s n d)").rearrange(
        "(t p c n) -> t p c n", t=8, p=128, c=NCH)

    io["x1s"] = nc.dram_tensor("x1s", [NTILE, 128, NCH, TT], F32).ap()
    io["hidden_s"] = nc.dram_tensor("hidden_s", [NTILE, 128, NCH, TT], F32).ap()
    io["q2s"] = nc.dram_tensor("q2s", [NTILE, 128, NCH, TT], BF16).ap()
    io["ln2s"] = nc.dram_tensor("ln2s", [NTILE, 128, NCH, TT], BF16).ap()
    io["accs"] = nc.dram_tensor("accs", [NTILE, 128, NCH, TT], F32).ap()
    io["kvc"] = nc.dram_tensor("kvc", [512, 4096], BF16).ap()
    io["gath"] = nc.dram_tensor("gath", [4096, 4096], BF16).ap()

    with ExitStack() as ctx:
        E = ctx.enter_context
        C = {}
        for nm in ["ld", "gw", "pe", "dv", "ac", "pl", "sp", "cc", "g2", "lk", "lv", "gs",
                   "wq0", "wq1", "w10", "w11", "sx0", "sx1", "so0", "so1"]:
            C[nm] = Ctr(E(nc.semaphore(nm)), nm)

        pb = [E(nc.psum_tensor(f"pb{i}", [128, TT], F32)) for i in range(8)]

        sb = {}
        for nm, shp, dt in [
            ("ident_bf", [128, 128], BF16), ("ident_f32", [128, 128], F32),
            ("ones1_f32", [1, 128], F32),
            ("ones128c_f32", [128, 1], F32), ("ones128c_bf", [128, 1], BF16),
            ("sel16", [16, 8 * 128], BF16), ("indT", [128, 8 * 16], BF16),
            ("esel_sb", [8, 8 * 128], BF16),
            ("wq22_bf", [128, 22], BF16),
            ("prior_row_sb", [1, 176], F32), ("prior_bc", [128, 176], F32),
            ("bqkv_pc", [128, 24], F32), ("bgate_pc", [128, 8], F32),
            ("bout_pc", [128, 8], F32),
            ("ln1w_pc", [128, 8], F32), ("ln1b_pc", [128, 8], F32),
            ("ln2w_pc", [128, 8], F32), ("ln2b_pc", [128, 8], F32),
            ("mask_sb", [16, 32], F32),
            ("wrout_bf", [128, 8, 8], BF16),
            ("c0_sp", [16, 1], F32), ("byp", [16, 1], F32), ("byp2", [16, 1], F32),
            ("fl_v", [1, 128], F32), ("fl_a", [1, 128], F32),
            ("w_em", [8, NTILE * TT], BF16),
        ]:
            sb[nm] = E(nc.sbuf_tensor("s_" + nm, shp, dt))

        block = E(nc.Block())
        P = Prog(nc, block, C, pb)
        P.fl_v = sb["fl_v"]
        P.fl_a = sb["fl_a"]

        def setup_sync(sync):
            for nm in ["ident_bf", "ident_f32", "ones1_f32", "sel16", "indT",
                       "bqkv_pc", "bgate_pc", "bout_pc", "prior_row_sb",
                       "ln1w_pc", "ln1b_pc", "ln2w_pc", "ln2b_pc", "mask_sb",
                       "esel_sb"]:
                key = {"prior_row_sb": "prior_row", "esel_sb": "esel",
                       "mask_sb": "mask_pc"}.get(nm, nm)
                C["ld"].inc(sync.dma_start(out=sb[nm][:], in_=io[key]), 16)
            C["ld"].inc(sync.dma_start(out=sb["byp"][:], in_=io["bypass16"]), 16)
            sync.wait_ge(C["ld"].sem, C["ld"].v)
        block.sync(setup_sync)
        ld_const = C["ld"].v

        def setup_gp(g):
            C["gs"].inc(g.dma_start(out=sb["wq22_bf"][:], in_=io["wq22"]), 16)
            C["gs"].inc(g.dma_start(
                out=sb["ones128c_bf"][:],
                in_=io["ones1_f32"].rearrange("a p -> p a")), 16)
            C["gs"].inc(g.dma_start(
                out=sb["ones128c_f32"][:],
                in_=io["ones1_f32"].rearrange("a p -> p a")), 16)
            C["gs"].inc(g.dma_start(out=sb["wrout_bf"][:], in_=io["wrout_pc"]), 16)
        block.gpsimd(setup_gp)
        gw_const = C["gs"].v

        def setup_pe(tensor):
            P.wait(tensor, "tensor", C["ld"], ld_const)
            C["pe"].inc(tensor.matmul(pb[0][:, 0:176], sb["ones1_f32"][:],
                                      sb["prior_row_sb"][:], start=True, stop=True))
        block.tensor(setup_pe)
        pe_prior = C["pe"].v

        def setup_v(vector):
            P.wait(vector, "vector", C["pe"], pe_prior)
            C["dv"].inc(vector.tensor_copy(sb["prior_bc"][:], pb[0][:, 0:176]))
        block.vector(setup_v)
        P.set_bank_guard(0, C["dv"], C["dv"].v)

        def setup_sp1(scalar):
            P.wait(scalar, "scalar", C["ld"], ld_const)
            scalar.activation(sb["byp2"][:], sb["byp"][:], AF.Abs)
            scalar.drain()
            scalar.activation(sb["byp2"][:], sb["byp2"][:], AF.Exp, scale=-1.0)
            scalar.drain()
            scalar.activation(sb["byp2"][:], sb["byp2"][:], AF.Ln, bias=1.0)
            C["ac"].inc(P.aflush(scalar))
        block.scalar(setup_sp1)
        ac_sp = C["ac"].v

        def setup_sp2(vector):
            P.wait(vector, "vector", C["ac"], ac_sp)
            vector.tensor_scalar(sb["c0_sp"][:], sb["byp"][:], 0.0, None, ALU.max)
            P.vspace(vector)
            vector.tensor_tensor(sb["c0_sp"][:], sb["c0_sp"][:], sb["byp2"][:], ALU.add)
            P.vspace(vector)
            vector.tensor_scalar(sb["c0_sp"][:], sb["c0_sp"][:], float(D4[0]), None, ALU.add)
            C["dv"].inc(P.vflush(vector))
        block.vector(setup_sp2)
        dv_setup = C["dv"].v

        def gate_consts(tensor):
            P.wait(tensor, "tensor", C["dv"], dv_setup)
            P.wait(tensor, "tensor", C["ld"], ld_const)
        block.tensor(gate_consts)

        with ExitStack() as actx:
            A = actx.enter_context
            sb["x_tm"] = [A(nc.sbuf_tensor("x_tm0", [128, 4, DM], F32))]
            sb["xn_tm"] = A(nc.sbuf_tensor("xn_tm", [128, 4, DM], BF16))
            sb["xn_fm"] = A(nc.sbuf_tensor("xn_fm", [128, NCH, TT], BF16))
            sb["q_fm"] = A(nc.sbuf_tensor("q_fm", [128, NCH, TT], BF16))
            sb["k_buf"] = A(nc.sbuf_tensor("k_buf", [128, NCH, SEQ], BF16))
            sb["v_buf"] = A(nc.sbuf_tensor("v_buf", [128, NCH, SEQ], BF16))
            sb["prod"] = A(nc.sbuf_tensor("prod", [128, NCH, TT], BF16))
            sb["acc"] = A(nc.sbuf_tensor("acc", [128, NCH, TT], F32))
            sb["gate_sb"] = A(nc.sbuf_tensor("gate_sb", [128, NCH, TT], BF16))
            sb["mtmp"] = [A(nc.sbuf_tensor(f"mtmp{i}", [128, TT], BF16)) for i in range(2)]
            sb["w16_sb"] = A(nc.sbuf_tensor("w16_sb", [16, TT], BF16))
            sb["y_sb"] = A(nc.sbuf_tensor("y_sb", [128, TT], F32))
            sb["x1st"] = [A(nc.sbuf_tensor(f"x1st{i}", [128, TT], F32)) for i in range(2)]
            sb["wqkv_oc"] = [A(nc.sbuf_tensor(f"wqkv_oc{i}", [128, NCH, 128], BF16))
                             for i in range(2)]
            sb["lnmean"] = A(nc.sbuf_tensor("lnmean", [128, 4], F32))
            sb["lnmv"] = A(nc.sbuf_tensor("lnmv", [128, 4, 2], F32))
            sb["lnvar"] = A(nc.sbuf_tensor("lnvar", [128, 4], F32))
            sb["lnsd"] = A(nc.sbuf_tensor("lnsd", [128, 4], F32))
            sb["lnrstd"] = A(nc.sbuf_tensor("lnrstd", [128, 4], F32))
            sb["qo_sb"] = A(nc.sbuf_tensor("qo_sb", [128, 176], F32))
            sb["qo_e"] = A(nc.sbuf_tensor("qo_e", [128, 176], F32))
            sb["qo_x"] = A(nc.sbuf_tensor("qo_x", [128, 176], F32))
            sb["qo_m"] = A(nc.sbuf_tensor("qo_m", [128, 16], F32))
            sb["qo_s"] = A(nc.sbuf_tensor("qo_s", [128, 16], F32))
            sb["qo_r"] = A(nc.sbuf_tensor("qo_r", [128, 16], F32))
            sb["qo_rv"] = A(nc.sbuf_tensor("qo_rv", [128, 3, 16], F32))
            sb["g123_tm"] = A(nc.sbuf_tensor("g123_tm", [128, 4, 528], BF16))
            sb["G_hm"] = A(nc.sbuf_tensor("G_hm", [128, 5, TT], BF16))
            sb["mbig_sb"] = A(nc.sbuf_tensor("mbig_sb", [128, 5, 11 * 128], BF16))
            sb["coefS"] = A(nc.sbuf_tensor("coefS", [128, 11, TT], BF16))

            def setup_gp2(g):
                C["gs"].inc(g.dma_start(
                    out=sb["mbig_sb"][:],
                    in_=io["mbig"].rearrange("(rb p) c -> p rb c", p=128)), 16)
            block.gpsimd(setup_gp2)
            gw_mbig = C["gs"].v

            def gate_mbig(tensor):
                P.wait(tensor, "tensor", C["gs"], gw_mbig)
            block.tensor(gate_mbig)

            _emit_phase_a(P, io, sb)

            def a_end(vector):
                P.wait(vector, "vector", C["sp"], C["sp"].v)
                P.wait(vector, "vector", C["sx0"], C["sx0"].v)
                P.wait(vector, "vector", C["sx1"], C["sx1"].v)
                C["dv"].inc(P.vflush(vector))
            block.vector(a_end)

            def a_end_a(scalar):
                C["ac"].inc(P.aflush(scalar))
            block.scalar(a_end_a)

            def a_end_t(tensor):
                C["pe"].inc(tensor.drain())
            block.tensor(a_end_t)
            sb["phasea_dv"] = C["dv"].v
            sb["phasea_ac"] = C["ac"].v
            sb["phasea_pe"] = C["pe"].v

        if _STAGE in ("full", "B", "C1", "C2"):
            with ExitStack() as bctx:
                B = bctx.enter_context
                sb["xb1"] = B(nc.sbuf_tensor("xb1", [128, NCH, TT], F32))
                sb["hidf"] = B(nc.sbuf_tensor("hidf", [128, NCH, TT], F32))
                sb["tmpf"] = B(nc.sbuf_tensor("tmpf", [128, NCH, TT], F32))
                sb["sqscr"] = B(nc.sbuf_tensor("sqscr", [128, NCH, TT], BF16))
                sb["lnx_bf"] = B(nc.sbuf_tensor("lnx_bf", [128, NCH, TT], BF16))
                sb["q2t"] = B(nc.sbuf_tensor("q2t", [128, NCH, TT], BF16))
                sb["kvstage"] = B(nc.sbuf_tensor("kvstage", [128, 4, TT], BF16))
                sb["wb2"] = [B(nc.sbuf_tensor(f"wb2{i}", [128, NCH, 128], BF16))
                             for i in range(2)]
                sb["rowM"] = B(nc.sbuf_tensor("rowM", [1, TT], F32))
                sb["rowT"] = B(nc.sbuf_tensor("rowT", [1, TT], F32))
                sb["rowV"] = B(nc.sbuf_tensor("rowV", [1, TT], F32))
                sb["rowSD"] = B(nc.sbuf_tensor("rowSD", [1, TT], F32))
                sb["rowA"] = B(nc.sbuf_tensor("rowA", [1, TT], F32))
                sb["rowB"] = B(nc.sbuf_tensor("rowB", [1, TT], F32))
                sb["A_s"] = B(nc.sbuf_tensor("A_s", [128, TT], F32))
                sb["B_s"] = B(nc.sbuf_tensor("B_s", [128, TT], F32))
                sb["scores"] = B(nc.sbuf_tensor("scores", [16, 16, TT], BF16))
                sb["esc"] = B(nc.sbuf_tensor("esc", [16, 16, TT], BF16))
                sb["en"] = B(nc.sbuf_tensor("en", [16, 16, TT], BF16))
                sb["m16"] = B(nc.sbuf_tensor("m16", [16, TT], F32))
                sb["s16"] = B(nc.sbuf_tensor("s16", [16, TT], F32))
                sb["r16"] = B(nc.sbuf_tensor("r16", [16, TT], F32))
                sb["kdup"] = B(nc.sbuf_tensor("kdup", [128, 4, TT], BF16))
                sb["vdup"] = B(nc.sbuf_tensor("vdup", [128, 4, TT], BF16))
                sb["prod2"] = B(nc.sbuf_tensor("prod2", [128, NCH, TT], BF16))
                sb["attn_acc"] = B(nc.sbuf_tensor("attn_acc", [128, NCH, TT], F32))
                sb["atmp"] = [B(nc.sbuf_tensor(f"atmp{i}", [128, TT], BF16)) for i in range(2)]
                sb["lg8"] = B(nc.sbuf_tensor("lg8", [8, TT], F32))
                sb["lgt"] = B(nc.sbuf_tensor("lgt", [128, 4, 8], F32))
                sb["m8"] = B(nc.sbuf_tensor("m8", [128, 8], F32))
                sb["ind8"] = B(nc.sbuf_tensor("ind8", [128, 8], F32))
                sb["negm"] = B(nc.sbuf_tensor("negm", [128, 1], F32))
                sb["e8"] = B(nc.sbuf_tensor("e8", [128, 8], F32))
                sb["e2b"] = B(nc.sbuf_tensor("e2b", [128, 1], F32))
                sb["rden"] = B(nc.sbuf_tensor("rden", [128, 1], F32))
                sb["w8"] = B(nc.sbuf_tensor("w8", [128, 8], F32))
                sb["lgt_dbg"] = B(nc.sbuf_tensor("lgt_dbg", [128, NTILE * 4, 8], F32))
                sb["b2_reuse_guard"] = [0]
                sb["kdup_guard"] = [0]
                sb["vdup_guard"] = [0]
                sb["sp_tile_guard"] = [C["sp"].v]

                _emit_phase_b(P, io, sb)

                if _STAGE == "C1":
                    def bdump(g):
                        P.wait(g, "gpsimd", C["sp"], C["sp"].v)
                        for t in range(NTILE):
                            C["gw"].inc(g.dma_start(out=io["out_flat"][t],
                                                    in_=io["ln2s"][t]), 16)
                    block.gpsimd(bdump)

                    def bdump2(g):
                        P.wait(g, "gpsimd", C["dv"], sb["b2_reuse_guard"][0])
                        C["gw"].inc(g.dma_start(
                            out=io["out"].rearrange("s n d -> (s n d)")
                            .rearrange("(z r c) -> z r c", z=2, r=8)[1, :, 0:NTILE * TT],
                            in_=sb["w_em"][:]), 16)
                        C["gw"].inc(g.dma_start(
                            out=io["out"].rearrange("s n d -> (s n d)")
                            .rearrange("(z r c) -> z r c", z=2, r=64)[1, 32:64, 0:NTILE * 4 * 8],
                            in_=sb["lgt_dbg"][0:32, :, :].rearrange("p a e -> p (a e)")), 16)
                    block.gpsimd(bdump2)

                def b_end(vector):
                    P.wait(vector, "vector", C["sp"], C["sp"].v)
                    P.wait(vector, "vector", C["gw"], C["gw"].v)
                    C["dv"].inc(P.vflush(vector))
                block.vector(b_end)

                def b_end_a(scalar):
                    C["ac"].inc(P.aflush(scalar))
                block.scalar(b_end_a)

                def b_end_t(tensor):
                    C["pe"].inc(tensor.drain())
                block.tensor(b_end_t)

            with ExitStack() as cctx:
              if _STAGE in ("full", "C2"):
                  Cx = cctx.enter_context
                  sb["acc_h"] = Cx(nc.sbuf_tensor("acc_h", [128, NCH, 4 * TT], F32))
                  sb["w2_sb"] = Cx(nc.sbuf_tensor("w2_sb", [128, 32, DM], BF16))
                  sb["w1_sb"] = [Cx(nc.sbuf_tensor(f"w1_sb{i}", [128, NCH, 4, 128], BF16))
                                 for i in range(2)]
                  sb["ln2t"] = [Cx(nc.sbuf_tensor("ln2t0", [128, NCH, TT], BF16))]
                  sb["act_w"] = Cx(nc.sbuf_tensor("act_w", [128, 32, TT], BF16))
                  sb["wrep"] = Cx(nc.sbuf_tensor("wrep", [128, 4, TT], BF16))
                  sb["ctmp"] = [Cx(nc.sbuf_tensor("ctmp0", [128, TT], BF16))]
                  sb["ost"] = [Cx(nc.sbuf_tensor(f"ost{i}", [128, 128], F32)) for i in range(2)]
                  if _STAGE == "C2":
                      sb["ydump"] = Cx(nc.sbuf_tensor("ydump", [128, TT], BF16))
                  sb["ln2_hist"] = []
                  sb["prev_y_pe"] = [0]
                  sb["phaseb_ac"] = [C["ac"].v]
                  sb["phaseb_pe"] = [C["pe"].v]
                  sb["phaseb_dv"] = [C["dv"].v]
                  sb["out_guard"] = [0]
                  sb["ost_guard"] = [None, None]
                  sb["phaseb_sp"] = [C["sp"].v]

                  _emit_phase_c(P, io, sb)

        def fin(g):
            for nm in ["sp", "gw", "sx0", "sx1", "so0", "so1", "gs", "wq0", "wq1",
                       "w10", "w11", "g2", "lk", "lv", "ld"]:
                P.wait(g, "gpsimd", C[nm], C[nm].v)
        block.gpsimd(fin)

    return nc


def kernel(**inputs):
    in_maps = _prep_host(inputs)
    nc = build_program()
    res = run_bass_kernel_spmd(nc, in_maps, core_ids=list(range(NCORE)))
    out = np.empty((16, SEQ, DM), np.float32)
    for cid in range(NCORE):
        o = res.results[cid]["out"]
        out[cid] = o[0]
        out[15 - cid] = o[1]
    return out


# version 05 (final)
# speedup vs baseline: 1.0028x; 1.0028x over previous
import numpy as np
import ml_dtypes
from contextlib import ExitStack

import concourse.bass as bass
import concourse.mybir as mybir
from concourse.bass_utils import run_bass_kernel_spmd

F32 = mybir.dt.float32
BF16 = mybir.dt.bfloat16
AF = mybir.ActivationFunctionType
ALU = mybir.AluOpType
AX = mybir.AxisListType

D4 = [0.4829629131445341, 0.8365163037378079, 0.2241438680420134, -0.1294095225512604]
OFFS = [1, 2, 3, 4, 6, 8, 12, 16, 24, 32, 48, 64, 96, 128, 192, 256, 384, 512, 768, 1024, 1536]
EPS = 1e-5
NCORE = 8
SEQ = 2048
DM = 1024
NCH = 8
TT = 512
NTILE = 8
BF16NP = ml_dtypes.bfloat16

_STAGE = "full"
_DBG_TILE = -1
_MAX_OFF = 4096
_NEXP = 8



def _prep_host(inp):
    f32 = np.float32
    c = np.ascontiguousarray

    shared = {}
    shared["wqkv_t"] = c(np.asarray(inp["L0_attn_W_qkv"]).T)
    shared["bqkv_pc"] = c(np.asarray(inp["L0_attn_b_qkv"]).reshape(24, 128).T)
    shared["wgate_t"] = c(np.asarray(inp["L0_attn_W_gate"]).T)
    shared["bgate_pc"] = c(np.asarray(inp["L0_attn_b_gate"]).reshape(8, 128).T)
    shared["wout_t"] = c(np.asarray(inp["L0_attn_W_out"]).T)
    shared["bout_pc"] = c(np.asarray(inp["L0_attn_b_out"]).reshape(8, 128).T)

    wq = np.asarray(inp["L0_attn_W_qscale"])
    wq22 = np.zeros((128, 22), f32)
    wq22[0:64, 0:11] = wq.T
    wq22[64:128, 11:22] = wq.T
    shared["wq22"] = wq22
    prior = np.asarray(inp["L0_attn_scale_gain"])
    shared["prior_row"] = c(prior.T.reshape(1, 176))
    shared["bypass16"] = c(np.asarray(inp["L0_attn_identity_bypass"]).reshape(16, 1))

    shared["ln1w_pc"] = c(np.asarray(inp["L0_ffn_ln1_weight"]).reshape(8, 128).T)
    shared["ln1b_pc"] = c(np.asarray(inp["L0_ffn_ln1_bias"]).reshape(8, 128).T)
    shared["ln2w_pc"] = c(np.asarray(inp["L0_ffn_ln2_weight"]).reshape(8, 128).T)
    shared["ln2b_pc"] = c(np.asarray(inp["L0_ffn_ln2_bias"]).reshape(8, 128).T)
    qkvw = np.asarray(inp["L0_ffn_qkv_weight"])
    rows = []
    for h in range(16):
        kap, j = h // 4, h % 4
        rows.append(qkvw[kap * 384 + j * 64: kap * 384 + (j + 1) * 64])
    for kap in range(4):
        rows.append(qkvw[kap * 384 + 256: kap * 384 + 320])
    for kap in range(4):
        rows.append(qkvw[kap * 384 + 320: kap * 384 + 384])
    shared["wqkv2_t"] = c(np.concatenate(rows, 0).T)
    shared["wproj_t"] = c(np.asarray(inp["L0_ffn_proj_weight"]).T)
    shared["wrout"] = c(np.asarray(inp["L0_ffn_router_weight"]))
    shared["wrout_pc"] = c(np.asarray(inp["L0_ffn_router_weight"]).reshape(8, 128, 8).transpose(1, 0, 2))
    shared["moe_w1"] = c(np.asarray(inp["L0_ffn_moe_w1"]))
    shared["moe_w2"] = c(np.asarray(inp["L0_ffn_moe_w2"]))

    shared["ident_bf"] = np.eye(128, dtype=BF16NP)
    shared["ident_f32"] = np.eye(128, dtype=f32)
    shared["ones1_f32"] = np.ones((1, 128), f32)
    sel16 = np.zeros((16, 8 * 128), BF16NP)
    indT = np.zeros((128, 8 * 16), BF16NP)
    for ch in range(8):
        for p in range(128):
            sel16[2 * ch + p // 64, ch * 128 + p] = 1
            indT[p, ch * 16 + 2 * ch + p // 64] = 1
    shared["sel16"] = sel16
    shared["indT"] = indT
    esel = np.zeros((8, 8 * 128), BF16NP)
    for e in range(8):
        esel[e, e * 128:(e + 1) * 128] = 1
    shared["esel"] = esel

    mbig = np.zeros((640, 11 * 128), BF16NP)

    def off_of(v, j):
        if v == 0:
            return 1 << j
        if v == 1:
            return (1 << (j + 1)) if j <= 9 else -1
        return 3 * (1 << j) if j <= 9 else -1

    for v in range(3):
        for h in range(16):
            for j in range(11):
                o = off_of(v, j)
                if o in OFFS:
                    i = OFFS.index(o)
                    mbig[v * 176 + h * 11 + j, (i // 2) * 128 + (i % 2) * 64 + h] = 1
    shared["mbig"] = mbig

    x_full = np.asarray(inp["L0_attn_x"])
    in_maps = []
    for cid in range(NCORE):
        m = dict(shared)
        m["xs"] = c(np.stack([x_full[cid], x_full[15 - cid]]))
        mask = np.zeros((16, 32), f32)
        for slot, sq in ((0, cid), (1, 15 - cid)):
            for t in range(16):
                if t > sq:
                    mask[:, slot * 16 + t] = -1e30
        m["mask_pc"] = mask
        in_maps.append(m)
    return in_maps



class Ctr:
    __slots__ = ("sem", "v", "name")

    def __init__(self, sem, name):
        self.sem = sem
        self.v = 0
        self.name = name

    def inc(self, inst, n=1):
        inst.then_inc(self.sem, n)
        self.v += n
        return self.v


class Prog:
    def __init__(self, nc, block, ctrs, banks):
        self.nc = nc
        self.block = block
        self.c = ctrs
        self.banks = banks
        self.bank_guard = [None] * len(banks)
        self.last_wait = {}

    def wait(self, eng, ename, ctr, val):
        if val <= 0:
            return
        k = (ename, ctr.name)
        if self.last_wait.get(k, -1) >= val:
            return
        eng.wait_ge(ctr.sem, val)
        self.last_wait[k] = val

    def wait_bank(self, eng, ename, bi):
        g = self.bank_guard[bi]
        if g is not None:
            self.wait(eng, ename, g[0], g[1])

    def set_bank_guard(self, bi, ctr, val):
        self.bank_guard[bi] = (ctr, val)

    def vflush(self, vector):
        return vector.drain()

    def aflush(self, scalar):
        return scalar.drain()

    def vspace(self, vector):
        vector.drain()



def _emit_phase_a(P, io, sb):
    block = P.block
    C = P.c
    pb = P.banks
    xs, x1s = io["xs"], io["x1s"]

    wqkv_guard = {}
    kv_mixer_guard = [0, 0]
    xbuf_free = {}

    for t in range(NTILE):
        si, n0 = t // 4, (t % 4) * TT
        xb = sb["x_tm"][0]
        seq_start = (t % 4 == 0)

        def s0(sync, t=t, si=si, n0=n0, xb=xb):
            if t >= 1:
                P.wait(sync, "sync", C["pe"], xbuf_free[t - 1])
            src = xs[si, n0:n0 + TT, :].rearrange("(tb p) d -> p tb d", p=128)
            C["ld"].inc(sync.dma_start(out=xb[:], in_=src), 16)
        block.sync(s0)
        ld_done = C["ld"].v

        def s1sq(scalar, xb=xb, ld_done=ld_done):
            P.wait(scalar, "scalar", C["ld"], ld_done)
            for tb in range(4):
                scalar.activation(sb["xn_tm"][:, tb, :], xb[:, tb, :], AF.Square)
            C["ac"].inc(P.aflush(scalar))
        block.scalar(s1sq)
        ac_sq = C["ac"].v

        def s1v(vector, xb=xb, ld_done=ld_done, ac_sq=ac_sq):
            P.wait(vector, "vector", C["ld"], ld_done)
            mv = sb["lnmv"]
            for tb in range(4):
                vector.tensor_reduce(mv[:, tb, 0:1], xb[:, tb, :], AX.X, ALU.add)
            P.wait(vector, "vector", C["ac"], ac_sq)
            for tb in range(4):
                vector.tensor_reduce(mv[:, tb, 1:2], sb["xn_tm"][:, tb, :], AX.X, ALU.add)
            P.vspace(vector)
            vector.tensor_scalar(sb["lnmean"][:], mv[:, :, 0], 1.0 / DM, None, ALU.mult)
            P.vspace(vector)
            vector.tensor_tensor(sb["lnsd"][:], sb["lnmean"][:], sb["lnmean"][:], ALU.mult)
            vector.tensor_scalar(sb["lnvar"][:], mv[:, :, 1], 1.0 / DM, None, ALU.mult)
            P.vspace(vector)
            vector.tensor_tensor(sb["lnvar"][:], sb["lnvar"][:], sb["lnsd"][:], ALU.subtract)
            P.vspace(vector)
            vector.tensor_scalar(sb["lnvar"][:], sb["lnvar"][:], EPS, None, ALU.add)
            C["dv"].inc(P.vflush(vector))
        block.vector(s1v)
        dv_var = C["dv"].v

        def s1a(scalar, dv_var=dv_var):
            P.wait(scalar, "scalar", C["dv"], dv_var)
            scalar.activation(sb["lnsd"][:], sb["lnvar"][:], AF.Sqrt)
            C["ac"].inc(P.aflush(scalar))
        block.scalar(s1a)
        ac_sd = C["ac"].v

        def s1v2(vector, xb=xb, ac_sd=ac_sd):
            P.wait(vector, "vector", C["ac"], ac_sd)
            vector.reciprocal(sb["lnrstd"][:], sb["lnsd"][:])
            P.vspace(vector)
            for tb in range(4):
                vector.tensor_scalar(
                    sb["xn_tm"][:, tb, :], xb[:, tb, :],
                    sb["lnmean"][:, tb:tb + 1], sb["lnrstd"][:, tb:tb + 1],
                    ALU.subtract, ALU.mult)
            C["dv"].inc(P.vflush(vector))
        block.vector(s1v2)
        dv_xn = C["dv"].v

        dv_evac_base = C["dv"].v

        def s2t(tensor, dv_xn=dv_xn, dv_evac_base=dv_evac_base):
            P.wait(tensor, "tensor", C["dv"], dv_xn)
            for j in range(32):
                tb, ch = j // 8, j % 8
                bi = j % 2
                if j >= 2:
                    P.wait(tensor, "tensor", C["dv"], dv_evac_base + j - 1)
                else:
                    P.wait_bank(tensor, "tensor", bi)
                pv = pb[bi].ap().bitcast(BF16)
                C["pe"].inc(tensor.transpose(
                    pv[:, 0:128], sb["xn_tm"][:, tb, ch * 128:(ch + 1) * 128],
                    sb["ident_bf"][:]))
        pe_t_base = C["pe"].v
        block.tensor(s2t)

        def s2v(vector, pe_t_base=pe_t_base):
            for j in range(32):
                tb, ch = j // 8, j % 8
                bi = j % 2
                P.wait(vector, "vector", C["pe"], pe_t_base + j + 1)
                pv = pb[bi].ap().bitcast(BF16)
                C["dv"].inc(vector.tensor_copy(
                    sb["xn_fm"][:, ch, tb * 128:(tb + 1) * 128], pv[:, 0:128]))
            C["dv"].inc(P.vflush(vector))
        block.vector(s2v)
        dv_xnfm = C["dv"].v
        P.set_bank_guard(0, C["dv"], dv_xnfm - 1)
        P.set_bank_guard(1, C["dv"], dv_xnfm)

        for oc in range(24):
            gi = t * 40 + oc
            wbuf = sb["wqkv_oc"][gi % 2]

            wsem = C["wq0"] if gi % 2 == 0 else C["wq1"]

            def s3g(g, gi=gi, oc=oc, wbuf=wbuf, wsem=wsem):
                if gi - 2 in wqkv_guard:
                    P.wait(g, "gpsimd", C["pe"], wqkv_guard[gi - 2])
                src = io["wqkv_t"][:, oc * 128:(oc + 1) * 128].rearrange(
                    "(ic p) o -> p ic o", p=128)
                wsem.inc(g.dma_start(out=wbuf[:], in_=src), 16)
            block.gpsimd(s3g)
            gw_done = wsem.v

            bi = 2 + gi % 2

            def s3t(tensor, wbuf=wbuf, bi=bi, gw_done=gw_done, dv_xnfm=dv_xnfm, wsem=wsem):
                P.wait(tensor, "tensor", wsem, gw_done)
                P.wait(tensor, "tensor", C["dv"], dv_xnfm)
                P.wait_bank(tensor, "tensor", bi)
                for ic in range(NCH):
                    inst = tensor.matmul(pb[bi][:], wbuf[:, ic, :], sb["xn_fm"][:, ic, :],
                                         start=(ic == 0), stop=(ic == NCH - 1))
                C["pe"].inc(inst)
            block.tensor(s3t)
            wqkv_guard[gi] = C["pe"].v
            pe_mm = C["pe"].v

            def s3a(scalar, oc=oc, bi=bi, pe_mm=pe_mm, n0=n0, seq_start=seq_start, si=si):
                P.wait(scalar, "scalar", C["pe"], pe_mm)
                if oc < 8:
                    dst = sb["q_fm"][:, oc, :]
                else:
                    if seq_start and si == 1:
                        P.wait(scalar, "scalar", C["dv"], kv_mixer_guard[0])
                    if oc < 16:
                        dst = sb["k_buf"][:, oc - 8, n0:n0 + TT]
                    else:
                        dst = sb["v_buf"][:, oc - 16, n0:n0 + TT]
                inst = scalar.activation(
                    dst, pb[bi][:], AF.Identity, bias=sb["bqkv_pc"][:, oc:oc + 1])
                C["ac"].inc(P.aflush(scalar) if oc == 23 else inst)
            block.scalar(s3a)
            P.set_bank_guard(bi, C["ac"], C["ac"].v)
        ac_qkv = C["ac"].v

        for tb in range(4):
            def s4t(tensor, tb=tb, ac_qkv=ac_qkv):
                P.wait(tensor, "tensor", C["ac"], ac_qkv)
                P.wait_bank(tensor, "tensor", 4)
                for ch in range(NCH):
                    inst = tensor.matmul(pb[4][:, ch * 22:(ch + 1) * 22],
                                         sb["q_fm"][:, ch, tb * 128:(tb + 1) * 128],
                                         sb["wq22_bf"][:], start=True, stop=True)
                C["pe"].inc(inst)
            block.tensor(s4t)
            pe_qo = C["pe"].v

            def s4v(vector, pe_qo=pe_qo):
                P.wait(vector, "vector", C["pe"], pe_qo)
                qo = sb["qo_sb"]
                vector.tensor_tensor(qo[:], pb[4][:, 0:176], sb["prior_bc"][:], ALU.add)
                v176 = qo.rearrange("p (h s) -> p h s", s=11)
                vector.tensor_reduce(sb["qo_m"][:, :, None], v176, AX.X, ALU.max)
                P.vspace(vector)
                mb = sb["qo_m"][:, :, None].broadcast_to([128, 16, 11])
                vector.tensor_tensor(
                    sb["qo_e"].rearrange("p (h s) -> p h s", s=11), v176, mb, ALU.subtract)
                C["dv"].inc(P.vflush(vector))
            block.vector(s4v)
            dv_sub = C["dv"].v
            P.set_bank_guard(4, C["dv"], dv_sub)

            def s4a(scalar, dv_sub=dv_sub):
                P.wait(scalar, "scalar", C["dv"], dv_sub)
                scalar.activation(sb["qo_x"][:], sb["qo_e"][:], AF.Exp)
                C["ac"].inc(P.aflush(scalar))
            block.scalar(s4a)
            ac_exp = C["ac"].v

            def s4v2(vector, tb=tb, ac_exp=ac_exp):
                P.wait(vector, "vector", C["ac"], ac_exp)
                e3 = sb["qo_x"].rearrange("p (h s) -> p h s", s=11)
                vector.tensor_reduce(sb["qo_s"][:, :, None], e3, AX.X, ALU.add)
                P.vspace(vector)
                vector.reciprocal(sb["qo_r"][:], sb["qo_s"][:])
                P.vspace(vector)
                for v in range(3):
                    vector.tensor_scalar(sb["qo_rv"][:, v, :], sb["qo_r"][:],
                                         float(D4[v + 1]), None, ALU.mult)
                P.vspace(vector)
                g = sb["g123_tm"]
                for v in range(3):
                    rb = sb["qo_rv"][:, v, :, None].broadcast_to([128, 16, 11])
                    inst = vector.tensor_tensor(
                        g[:, tb, v * 176:(v + 1) * 176].rearrange("p (h s) -> p h s", s=11),
                        e3, rb, ALU.mult)
                C["dv"].inc(inst)
            block.vector(s4v2)

        def s4fl(vector):
            C["dv"].inc(P.vflush(vector))
        block.vector(s4fl)
        dv_g = C["dv"].v

        dv_evac_base = C["dv"].v

        def s5t(tensor, dv_g=dv_g, dv_evac_base=dv_evac_base):
            P.wait(tensor, "tensor", C["dv"], dv_g)
            for j in range(20):
                tb, rb = j // 5, j % 5
                w = 128 if rb < 4 else 16
                bi = j % 2
                if j >= 2:
                    P.wait(tensor, "tensor", C["dv"], dv_evac_base + j - 1)
                else:
                    P.wait_bank(tensor, "tensor", bi)
                pv = pb[bi].ap().bitcast(BF16)
                C["pe"].inc(tensor.transpose(
                    pv[0:w, 0:128], sb["g123_tm"][:, tb, rb * 128:rb * 128 + w],
                    sb["ident_bf"][:]))
        pe_gt0 = C["pe"].v
        block.tensor(s5t)

        def s5v(vector, pe_gt0=pe_gt0):
            for j in range(20):
                tb, rb = j // 5, j % 5
                w = 128 if rb < 4 else 16
                bi = j % 2
                P.wait(vector, "vector", C["pe"], pe_gt0 + j + 1)
                pv = pb[bi].ap().bitcast(BF16)
                C["dv"].inc(vector.tensor_copy(
                    sb["G_hm"][0:w, rb, tb * 128:(tb + 1) * 128], pv[0:w, 0:128]))
            C["dv"].inc(P.vflush(vector))
        block.vector(s5v)
        dv_ghm = C["dv"].v
        P.set_bank_guard(0, C["dv"], dv_ghm - 1)
        P.set_bank_guard(1, C["dv"], dv_ghm)

        for blk in range(11):
            bi = 4 + blk % 2

            def s6t(tensor, blk=blk, bi=bi, dv_ghm=dv_ghm):
                P.wait(tensor, "tensor", C["dv"], dv_ghm)
                P.wait_bank(tensor, "tensor", bi)
                for rb in range(5):
                    w = 128 if rb < 4 else 16
                    inst = tensor.matmul(pb[bi][:],
                                         sb["mbig_sb"][0:w, rb, blk * 128:(blk + 1) * 128],
                                         sb["G_hm"][0:w, rb, :],
                                         start=(rb == 0), stop=(rb == 4))
                C["pe"].inc(inst)
            block.tensor(s6t)
            pe_cf = C["pe"].v

            def s6v(vector, blk=blk, bi=bi, pe_cf=pe_cf):
                P.wait(vector, "vector", C["pe"], pe_cf)
                C["dv"].inc(vector.tensor_copy(sb["coefS"][:, blk, :], pb[bi][:]))
            block.vector(s6v)
            P.set_bank_guard(bi, C["dv"], C["dv"].v)
        dv_coef = C["dv"].v

        offsets = [0] + [o for o in OFFS if o < n0 + TT and o <= _MAX_OFF]
        for o in offsets:
            lo = max(0, o - n0)
            nv = TT - lo

            def s7p(vector, o=o, lo=lo, nv=nv, n0=n0, ac_qkv=ac_qkv, dv_coef=dv_coef):
                P.wait(vector, "vector", C["ac"], ac_qkv)
                P.wait(vector, "vector", C["dv"], dv_coef)
                vector.tensor_tensor(
                    sb["prod"][:, :, lo:lo + nv],
                    sb["q_fm"][:, :, lo:lo + nv],
                    sb["k_buf"][:, :, n0 + lo - o:n0 + TT - o], ALU.mult)
                C["dv"].inc(P.vflush(vector))
            block.vector(s7p)
            dv_prod = C["dv"].v

            def s7t(tensor, lo=lo, nv=nv, dv_prod=dv_prod):
                P.wait(tensor, "tensor", C["dv"], dv_prod)
                P.wait_bank(tensor, "tensor", 5)
                for ch in range(NCH):
                    inst = tensor.matmul(pb[5][0:16, lo:lo + nv],
                                         sb["indT"][:, ch * 16:(ch + 1) * 16],
                                         sb["prod"][:, ch, lo:lo + nv],
                                         start=(ch == 0), stop=(ch == NCH - 1))
                C["pe"].inc(inst)
            block.tensor(s7t)
            pe_qk = C["pe"].v

            def s7w(vector, o=o, lo=lo, nv=nv, pe_qk=pe_qk):
                P.wait(vector, "vector", C["pe"], pe_qk)
                if o == 0:
                    vector.tensor_scalar(
                        sb["w16_sb"][:, lo:lo + nv], pb[5][0:16, lo:lo + nv],
                        sb["c0_sp"][:], None, ALU.mult)
                else:
                    i = OFFS.index(o)
                    vector.tensor_tensor(
                        sb["w16_sb"][:, lo:lo + nv],
                        sb["coefS"][(i % 2) * 64:(i % 2) * 64 + 16, i // 2, lo:lo + nv],
                        pb[5][0:16, lo:lo + nv], ALU.mult)
                C["dv"].inc(P.vflush(vector))
            block.vector(s7w)
            dv_w16 = C["dv"].v
            P.set_bank_guard(5, C["dv"], dv_w16)

            for ch in range(NCH):
                bi = 6 + ch % 2

                def s7r(tensor, ch=ch, bi=bi, lo=lo, nv=nv, dv_w16=dv_w16):
                    P.wait(tensor, "tensor", C["dv"], dv_w16)
                    P.wait_bank(tensor, "tensor", bi)
                    C["pe"].inc(tensor.matmul(
                        pb[bi][:, lo:lo + nv],
                        sb["sel16"][:, ch * 128:(ch + 1) * 128],
                        sb["w16_sb"][:, lo:lo + nv], start=True, stop=True))
                block.tensor(s7r)
                pe_rep = C["pe"].v

                def s7m(vector, o=o, ch=ch, bi=bi, lo=lo, nv=nv, n0=n0, pe_rep=pe_rep):
                    P.wait(vector, "vector", C["pe"], pe_rep)
                    vsl = sb["v_buf"][:, ch, n0 + lo - o:n0 + TT - o]
                    if o == 0:
                        C["dv"].inc(vector.tensor_tensor(
                            sb["acc"][:, ch, lo:lo + nv], pb[bi][:, lo:lo + nv],
                            vsl, ALU.mult))
                    else:
                        vector.tensor_tensor(
                            sb["mtmp"][ch % 2][:, lo:lo + nv], pb[bi][:, lo:lo + nv],
                            vsl, ALU.mult)
                        C["dv"].inc(vector.tensor_tensor(
                            sb["acc"][:, ch, lo:lo + nv], sb["acc"][:, ch, lo:lo + nv],
                            sb["mtmp"][ch % 2][:, lo:lo + nv], ALU.add))
                block.vector(s7m)
                P.set_bank_guard(bi, C["dv"], C["dv"].v)
        def s7fl(vector):
            C["dv"].inc(P.vflush(vector))
        block.vector(s7fl)
        dv_mixer = C["dv"].v
        if t % 4 == 3:
            kv_mixer_guard[si] = dv_mixer

        for ch in range(NCH):
            gi = t * 40 + 24 + ch
            wbuf = sb["wqkv_oc"][gi % 2]

            wsem = C["wq0"] if gi % 2 == 0 else C["wq1"]

            def s8g(g, gi=gi, ch=ch, wbuf=wbuf, wsem=wsem):
                if gi - 2 in wqkv_guard:
                    P.wait(g, "gpsimd", C["pe"], wqkv_guard[gi - 2])
                src_ = io["wgate_t"][:, ch * 128:(ch + 1) * 128].rearrange(
                    "(ic p) o -> p ic o", p=128)
                wsem.inc(g.dma_start(out=wbuf[:], in_=src_), 16)
            block.gpsimd(s8g)
            gw_g = wsem.v

            bi = 2 + ch % 2

            def s8t(tensor, bi=bi, wbuf=wbuf, gw_g=gw_g, dv_xnfm=dv_xnfm, wsem=wsem):
                P.wait(tensor, "tensor", wsem, gw_g)
                P.wait(tensor, "tensor", C["dv"], dv_xnfm)
                P.wait_bank(tensor, "tensor", bi)
                for ic in range(NCH):
                    inst = tensor.matmul(pb[bi][:], wbuf[:, ic, :],
                                         sb["xn_fm"][:, ic, :],
                                         start=(ic == 0), stop=(ic == NCH - 1))
                C["pe"].inc(inst)
            block.tensor(s8t)
            wqkv_guard[gi] = C["pe"].v
            pe_g = C["pe"].v

            def s8a(scalar, ch=ch, bi=bi, pe_g=pe_g):
                P.wait(scalar, "scalar", C["pe"], pe_g)
                inst = scalar.activation(
                    sb["gate_sb"][:, ch, :], pb[bi][:], AF.Sigmoid,
                    bias=sb["bgate_pc"][:, ch:ch + 1])
                C["ac"].inc(P.aflush(scalar) if ch == 7 else inst)
            block.scalar(s8a)
            P.set_bank_guard(bi, C["ac"], C["ac"].v)
        ac_gate = C["ac"].v

        if t == _DBG_TILE:
            def sdbg(sync, n0=n0, ac_qkv=ac_qkv, dv_g=dv_g, dv_coef=dv_coef,
                     dv_mixer=dv_mixer, ac_gate=ac_gate, dv_xnfm=dv_xnfm):
                P.wait(sync, "sync", C["ac"], ac_qkv)
                C["sp"].inc(sync.dma_start(out=io["dbg_q"], in_=sb["q_fm"][:]), 16)
                C["sp"].inc(sync.dma_start(out=io["dbg_kv"][0], in_=sb["k_buf"][:, :, n0:n0 + TT]), 16)
                C["sp"].inc(sync.dma_start(out=io["dbg_kv"][1], in_=sb["v_buf"][:, :, n0:n0 + TT]), 16)
                P.wait(sync, "sync", C["dv"], dv_xnfm)
                C["sp"].inc(sync.dma_start(out=io["dbg_xn"], in_=sb["xn_fm"][:]), 16)
                P.wait(sync, "sync", C["dv"], dv_g)
                C["sp"].inc(sync.dma_start(out=io["dbg_g"], in_=sb["g123_tm"][:]), 16)
                P.wait(sync, "sync", C["dv"], dv_coef)
                C["sp"].inc(sync.dma_start(out=io["dbg_coef"], in_=sb["coefS"][:]), 16)
                P.wait(sync, "sync", C["dv"], dv_mixer)
                C["sp"].inc(sync.dma_start(out=io["dbg_acc"], in_=sb["acc"][:]), 16)
                P.wait(sync, "sync", C["ac"], ac_gate)
                C["sp"].inc(sync.dma_start(out=io["dbg_gate"], in_=sb["gate_sb"][:]), 16)
            block.sync(sdbg)

        def s9v(vector, ac_gate=ac_gate, dv_mixer=dv_mixer):
            P.wait(vector, "vector", C["ac"], ac_gate)
            P.wait(vector, "vector", C["dv"], dv_mixer)
            vector.tensor_tensor(
                sb["prod"][:], sb["acc"][:], sb["gate_sb"][:], ALU.mult)
            C["dv"].inc(P.vflush(vector))
        block.vector(s9v)
        dv_gg = C["dv"].v

        x1st_guard = [0, 0]
        for oc in range(NCH):
            bi = 2 + oc % 2

            gi = t * 40 + 32 + oc
            wbuf = sb["wqkv_oc"][gi % 2]

            wsem = C["wq0"] if gi % 2 == 0 else C["wq1"]

            def s10g(g, gi=gi, oc=oc, wbuf=wbuf, wsem=wsem):
                if gi - 2 in wqkv_guard:
                    P.wait(g, "gpsimd", C["pe"], wqkv_guard[gi - 2])
                src_ = io["wout_t"][:, oc * 128:(oc + 1) * 128].rearrange(
                    "(ic p) o -> p ic o", p=128)
                wsem.inc(g.dma_start(out=wbuf[:], in_=src_), 16)
            block.gpsimd(s10g)
            gw_y = wsem.v

            def s10t(tensor, bi=bi, wbuf=wbuf, gw_y=gw_y, dv_gg=dv_gg, wsem=wsem):
                P.wait(tensor, "tensor", wsem, gw_y)
                P.wait(tensor, "tensor", C["dv"], dv_gg)
                P.wait_bank(tensor, "tensor", bi)
                for ic in range(NCH):
                    inst = tensor.matmul(pb[bi][:], wbuf[:, ic, :],
                                         sb["prod"][:, ic, :],
                                         start=(ic == 0), stop=(ic == NCH - 1))
                C["pe"].inc(inst)
            block.tensor(s10t)
            wqkv_guard[gi] = C["pe"].v
            pe_y = C["pe"].v

            def s10a(scalar, oc=oc, bi=bi, pe_y=pe_y):
                P.wait(scalar, "scalar", C["pe"], pe_y)
                scalar.activation(
                    sb["y_sb"][:], pb[bi][:], AF.Identity, bias=sb["bout_pc"][:, oc:oc + 1])
                C["ac"].inc(P.aflush(scalar))
            block.scalar(s10a)
            ac_y = C["ac"].v
            P.set_bank_guard(bi, C["ac"], ac_y)

            for tb in range(4):
                bj = tb % 2

                def s10x(tensor, oc=oc, tb=tb, bj=bj, xb=xb):
                    P.wait_bank(tensor, "tensor", bj)
                    C["pe"].inc(tensor.transpose(
                        pb[bj].ap()[:, 0:128], xb[:, tb, oc * 128:(oc + 1) * 128],
                        sb["ident_f32"][:]))
                block.tensor(s10x)
                pe_x = C["pe"].v

                def s10v(vector, oc=oc, tb=tb, bj=bj, pe_x=pe_x, ac_y=ac_y,
                         x1g=x1st_guard[oc % 2]):
                    if x1g != 0:
                        P.wait(vector, "vector", x1g[0], x1g[1])
                    P.wait(vector, "vector", C["pe"], pe_x)
                    P.wait(vector, "vector", C["ac"], ac_y)
                    C["dv"].inc(vector.tensor_tensor(
                        sb["x1st"][oc % 2][:, tb * 128:(tb + 1) * 128],
                        sb["y_sb"][:, tb * 128:(tb + 1) * 128],
                        pb[bj].ap()[:, 0:128], ALU.add))
                block.vector(s10v)
                P.set_bank_guard(bj, C["dv"], C["dv"].v)

            def s10fl(vector):
                C["dv"].inc(P.vflush(vector))
            block.vector(s10fl)
            dv_x1 = C["dv"].v

            xsem = C["sx0"] if oc % 2 == 0 else C["sx1"]

            def s10s(sync, t=t, oc=oc, dv_x1=dv_x1, xsem=xsem):
                P.wait(sync, "sync", C["dv"], dv_x1)
                xsem.inc(sync.dma_start(
                    out=x1s[t, :, oc, :], in_=sb["x1st"][oc % 2][:]), 16)
                if _STAGE == "A":
                    xsem.inc(sync.dma_start(
                        out=io["out_flat"][t, :, oc, :], in_=sb["x1st"][oc % 2][:]), 16)
            block.sync(s10s)
            x1st_guard[oc % 2] = (xsem, xsem.v)

        xbuf_free[t] = C["pe"].v



def _emit_ln_fm(P, sb, src, affine=None, out_bf=None, out_f32=None,
                dv_dep=0, ld_dep=0):
    block = P.block
    C = P.c
    pb = P.banks

    def a_sq(scalar):
        P.wait(scalar, "scalar", C["dv"], dv_dep)
        P.wait(scalar, "scalar", C["ld"], ld_dep)
        for ch in range(NCH):
            scalar.activation(sb["sqscr"][:, ch, :], src[:, ch, :], AF.Square)
        C["ac"].inc(P.aflush(scalar))
    block.scalar(a_sq)
    ac_sq = C["ac"].v

    def t_sum(tensor, ac_sq=ac_sq):
        P.wait(tensor, "tensor", C["ac"], ac_sq)
        P.wait_bank(tensor, "tensor", 4)
        P.wait_bank(tensor, "tensor", 5)
        for ch in range(NCH):
            tensor.matmul(pb[4][0:1, :], sb["ones128c_f32"][:], src[:, ch, :],
                          start=(ch == 0), stop=(ch == NCH - 1))
        for ch in range(NCH):
            inst = tensor.matmul(pb[5][0:1, :], sb["ones128c_bf"][:], sb["sqscr"][:, ch, :],
                                 start=(ch == 0), stop=(ch == NCH - 1))
        C["pe"].inc(inst)
    block.tensor(t_sum)
    pe_sum = C["pe"].v

    def v_rows(vector, pe_sum=pe_sum):
        P.wait(vector, "vector", C["pe"], pe_sum)
        vector.tensor_scalar(sb["rowM"][:], pb[4][0:1, :], 1.0 / DM, None, ALU.mult)
        P.vspace(vector)
        vector.tensor_tensor(sb["rowT"][:], sb["rowM"][:], sb["rowM"][:], ALU.mult)
        vector.tensor_scalar(sb["rowV"][:], pb[5][0:1, :], 1.0 / DM, None, ALU.mult)
        P.vspace(vector)
        vector.tensor_tensor(sb["rowV"][:], sb["rowV"][:], sb["rowT"][:], ALU.subtract)
        P.vspace(vector)
        vector.tensor_scalar(sb["rowV"][:], sb["rowV"][:], EPS, None, ALU.add)
        C["dv"].inc(P.vflush(vector))
    block.vector(v_rows)
    dv_rows = C["dv"].v
    P.set_bank_guard(4, C["dv"], dv_rows)
    P.set_bank_guard(5, C["dv"], dv_rows)

    def a_sd(scalar, dv_rows=dv_rows):
        P.wait(scalar, "scalar", C["dv"], dv_rows)
        scalar.activation(sb["rowSD"][:], sb["rowV"][:], AF.Sqrt)
        C["ac"].inc(P.aflush(scalar))
    block.scalar(a_sd)
    ac_sd = C["ac"].v

    def v_ab(vector, ac_sd=ac_sd):
        P.wait(vector, "vector", C["ac"], ac_sd)
        vector.reciprocal(sb["rowA"][:], sb["rowSD"][:])
        P.vspace(vector)
        vector.tensor_tensor(sb["rowB"][:], sb["rowM"][:], sb["rowA"][:], ALU.mult)
        P.vspace(vector)
        vector.tensor_scalar(sb["rowB"][:], sb["rowB"][:], -1.0, None, ALU.mult)
        C["dv"].inc(P.vflush(vector))
    block.vector(v_ab)
    dv_ab = C["dv"].v

    def t_rep(tensor, dv_ab=dv_ab):
        P.wait(tensor, "tensor", C["dv"], dv_ab)
        P.wait_bank(tensor, "tensor", 4)
        P.wait_bank(tensor, "tensor", 5)
        tensor.matmul(pb[4][:], sb["ones1_f32"][:], sb["rowA"][:], start=True, stop=True)
        inst = tensor.matmul(pb[5][:], sb["ones1_f32"][:], sb["rowB"][:], start=True, stop=True)
        C["pe"].inc(inst)
    block.tensor(t_rep)
    pe_rep = C["pe"].v

    def v_norm(vector, pe_rep=pe_rep, affine=affine, out_bf=out_bf, out_f32=out_f32):
        P.wait(vector, "vector", C["pe"], pe_rep)
        vector.tensor_copy(sb["A_s"][:], pb[4][:])
        vector.tensor_copy(sb["B_s"][:], pb[5][:])
        P.vspace(vector)
        ab = sb["A_s"][:, None, :].broadcast_to([128, NCH, TT])
        bb = sb["B_s"][:, None, :].broadcast_to([128, NCH, TT])
        tgt_plain = out_f32 if out_f32 is not None else (out_bf if affine is None else None)
        if tgt_plain is not None and affine is None:
            vector.tensor_tensor(sb["tmpf"][:], src, ab, ALU.mult)
            vector.tensor_tensor(tgt_plain, sb["tmpf"][:], bb, ALU.add)
        else:
            vector.tensor_tensor(sb["tmpf"][:], src, ab, ALU.mult)
            vector.tensor_tensor(sb["tmpf"][:], sb["tmpf"][:], bb, ALU.add)
            w_pc, b_pc = affine
            for ch in range(NCH):
                vector.tensor_scalar(out_bf[:, ch, :], sb["tmpf"][:, ch, :],
                                     w_pc[:, ch:ch + 1], b_pc[:, ch:ch + 1],
                                     ALU.mult, ALU.add)
        C["dv"].inc(P.vflush(vector))
    block.vector(v_norm)
    P.set_bank_guard(4, C["dv"], C["dv"].v)
    P.set_bank_guard(5, C["dv"], C["dv"].v)
    return C["dv"].v



def _emit_phase_b(P, io, sb):
    block = P.block
    C = P.c
    pb = P.banks
    wb_guard = {}

    def bgate_g(g):
        P.wait(g, "gpsimd", C["dv"], sb["phasea_dv"])
        P.wait(g, "gpsimd", C["ac"], sb["phasea_ac"])
        P.wait(g, "gpsimd", C["pe"], sb["phasea_pe"])
        P.wait(g, "gpsimd", C["sp"], C["sp"].v)
        P.wait(g, "gpsimd", C["sx0"], C["sx0"].v)
        P.wait(g, "gpsimd", C["sx1"], C["sx1"].v)
    block.gpsimd(bgate_g)

    def bgate_s(sync):
        P.wait(sync, "sync", C["dv"], sb["phasea_dv"])
        P.wait(sync, "sync", C["ac"], sb["phasea_ac"])
        P.wait(sync, "sync", C["pe"], sb["phasea_pe"])
    block.sync(bgate_s)

    for t in range(NTILE):
        si, n0 = t // 4, (t % 4) * TT

        def b0(sync, t=t):
            P.wait(sync, "sync", C["sp"], sb["sp_tile_guard"][0])
            P.wait(sync, "sync", C["sx0"], C["sx0"].v)
            P.wait(sync, "sync", C["sx1"], C["sx1"].v)
            src = io["x1s"][t].rearrange("p c n -> p c n")
            C["ld"].inc(sync.dma_start(out=sb["xb1"][:], in_=src), 16)
        block.sync(b0)
        ld_x1 = C["ld"].v

        def b0w(vector, t=t, ld_x1=ld_x1):
            P.wait(vector, "vector", C["ld"], ld_x1)
            P.wait(vector, "vector", C["sp"], sb["sp_tile_guard"][0])
        block.vector(b0w)

        def b0wa(scalar, t=t, ld_x1=ld_x1):
            P.wait(scalar, "scalar", C["ld"], ld_x1)
            P.wait(scalar, "scalar", C["sp"], sb["sp_tile_guard"][0])
        block.scalar(b0wa)

        dv_hid = _emit_ln_fm(P, sb, sb["xb1"][:], out_f32=sb["hidf"][:], ld_dep=ld_x1)
        dv_ln1 = _emit_ln_fm(P, sb, sb["hidf"][:],
                             affine=(sb["ln1w_pc"], sb["ln1b_pc"]), out_bf=sb["lnx_bf"],
                             dv_dep=dv_hid)

        def b1s(sync, t=t, dv_hid=dv_hid):
            P.wait(sync, "sync", C["dv"], dv_hid)
            C["sp"].inc(sync.dma_start(out=io["hidden_s"][t], in_=sb["hidf"][:]), 16)
        block.sync(b1s)

        for oc in range(12):
            gi = 1000 + t * 12 + oc
            wbuf = sb["wb2"][gi % 2]

            wsem = C["wq0"] if gi % 2 == 0 else C["wq1"]

            def b2g(g, gi=gi, oc=oc, wbuf=wbuf, wsem=wsem):
                if gi - 2 in wb_guard:
                    P.wait(g, "gpsimd", C["pe"], wb_guard[gi - 2])
                src_ = io["wqkv2_t"][:, oc * 128:(oc + 1) * 128].rearrange(
                    "(ic p) o -> p ic o", p=128)
                wsem.inc(g.dma_start(out=wbuf[:], in_=src_), 16)
            block.gpsimd(b2g)
            gw_done = wsem.v

            bi = 2 + gi % 2

            def b2t(tensor, wbuf=wbuf, bi=bi, gw_done=gw_done, dv_ln1=dv_ln1, wsem=wsem):
                P.wait(tensor, "tensor", wsem, gw_done)
                P.wait(tensor, "tensor", C["dv"], dv_ln1)
                P.wait_bank(tensor, "tensor", bi)
                for ic in range(NCH):
                    inst = tensor.matmul(pb[bi][:], wbuf[:, ic, :], sb["lnx_bf"][:, ic, :],
                                         start=(ic == 0), stop=(ic == NCH - 1))
                C["pe"].inc(inst)
            block.tensor(b2t)
            wb_guard[gi] = C["pe"].v
            pe_mm = C["pe"].v

            def b2a(scalar, oc=oc, bi=bi, pe_mm=pe_mm):
                P.wait(scalar, "scalar", C["pe"], pe_mm)
                if oc < 8:
                    inst = scalar.activation(sb["q2t"][:, oc, :], pb[bi][:],
                                             AF.Identity, bias=0.0, scale=0.125)
                else:
                    inst = scalar.activation(sb["kvstage"][:, oc - 8, :], pb[bi][:],
                                             AF.Identity)
                C["ac"].inc(P.aflush(scalar) if oc == 11 else inst)
            block.scalar(b2a)
            P.set_bank_guard(bi, C["ac"], C["ac"].v)
        ac_qkv2 = C["ac"].v

        def b3s(sync, t=t, si=si, n0=n0, ac_qkv2=ac_qkv2):
            P.wait(sync, "sync", C["ac"], ac_qkv2)
            C["sp"].inc(sync.dma_start(out=io["q2s"][t], in_=sb["q2t"][:]), 16)
            dst = io["kvc"].rearrange("(g p) c -> p g c", p=128)[:, :, si * SEQ + n0:si * SEQ + n0 + TT]
            C["sp"].inc(sync.dma_start(out=dst, in_=sb["kvstage"][:]), 16)
        block.sync(b3s)
        sb["sp_tile_guard"][0] = C["sp"].v
    sp_b1 = C["sp"].v

    def bcc(g, sp_b1=sp_b1):
        P.wait(g, "gpsimd", C["sp"], sp_b1)
        C["cc"].inc(g.collective_compute(
            "AllGather", ALU.bypass, replica_groups=[list(range(NCORE))],
            ins=[io["kvc"]], outs=[io["gath"]]))
    block.gpsimd(bcc)
    cc_done = C["cc"].v

    for t in range(NTILE):
        si, n0 = t // 4, (t % 4) * TT
        tmax = 8 if si == 0 else 16

        def c0(sync, t=t, cc_done=cc_done):
            P.wait(sync, "sync", C["cc"], cc_done)
            if t > 0:
                P.wait(sync, "sync", C["dv"], sb["b2_reuse_guard"][0])
            P.wait(sync, "sync", C["sp"], sb["sp_tile_guard"][0])
            C["ld"].inc(sync.dma_start(out=sb["q2t"][:], in_=io["q2s"][t]), 16)
            C["ld"].inc(sync.dma_start(out=sb["xb1"][:], in_=io["x1s"][t]), 16)
            C["ld"].inc(sync.dma_start(out=sb["hidf"][:], in_=io["hidden_s"][t]), 16)
        block.sync(c0)
        ld_t = C["ld"].v

        def c0w(vector, ld_t=ld_t):
            P.wait(vector, "vector", C["ld"], ld_t)
        block.vector(c0w)

        def c0wa(scalar, ld_t=ld_t):
            P.wait(scalar, "scalar", C["ld"], ld_t)
        block.scalar(c0wa)

        for tq in range(tmax):
            r = min(tq, 15 - tq)
            hh = 0 if tq == r else 1

            def c1k(sync, t=t, si=si, n0=n0, r=r, hh=hh, tq=tq, ld_t=ld_t):
                if (t, tq) != (0, 0):
                    P.wait(sync, "sync", C["dv"], sb["kdup_guard"][0])
                for kap in range(4):
                    for dup in range(2):
                        srcr = io["gath"][r * 512 + kap * 64: r * 512 + (kap + 1) * 64,
                                          hh * SEQ + n0: hh * SEQ + n0 + TT]
                        C["lk"].inc(sync.dma_start(
                            out=sb["kdup"][dup * 64:(dup + 1) * 64, kap, :], in_=srcr), 16)
            block.sync(c1k)
            ld_k = C["lk"].v

            def c1p(vector, ld_k=ld_k, ld_t=ld_t):
                P.wait(vector, "vector", C["lk"], ld_k)
                P.wait(vector, "vector", C["ld"], ld_t)
                kb = sb["kdup"][:, :, None, :].broadcast_to([128, 4, 2, TT])
                vector.tensor_tensor(
                    sb["prod2"].rearrange("p (a b) n -> p a b n", b=2),
                    sb["q2t"].rearrange("p (a b) n -> p a b n", b=2), kb, ALU.mult)
                C["dv"].inc(P.vflush(vector))
            block.vector(c1p)
            dv_prod = C["dv"].v
            sb["kdup_guard"][0] = dv_prod

            def c1t(tensor, dv_prod=dv_prod):
                P.wait(tensor, "tensor", C["dv"], dv_prod)
                P.wait_bank(tensor, "tensor", 5)
                for ch in range(NCH):
                    inst = tensor.matmul(pb[5][0:16, :],
                                         sb["indT"][:, ch * 16:(ch + 1) * 16],
                                         sb["prod2"][:, ch, :],
                                         start=(ch == 0), stop=(ch == NCH - 1))
                C["pe"].inc(inst)
            block.tensor(c1t)
            pe_sc = C["pe"].v

            def c1a(scalar, si=si, tq=tq, pe_sc=pe_sc):
                P.wait(scalar, "scalar", C["pe"], pe_sc)
                inst = scalar.activation(sb["scores"][0:16, tq, :], pb[5][0:16, :],
                                         AF.Identity,
                                         bias=sb["mask_sb"][:, si * 16 + tq: si * 16 + tq + 1])
                C["ac"].inc(P.aflush(scalar) if tq == tmax - 1 else inst)
            block.scalar(c1a)
            P.set_bank_guard(5, C["ac"], C["ac"].v)
        ac_sc = C["ac"].v

        def c2v(vector, tmax=tmax, ac_sc=ac_sc):
            P.wait(vector, "vector", C["ac"], ac_sc)
            s = sb["scores"]
            vector.tensor_tensor(sb["m16"][0:16, :], s[0:16, 0, :], s[0:16, 1, :], ALU.max)
            for tq in range(2, tmax):
                vector.tensor_tensor(sb["m16"][0:16, :], sb["m16"][0:16, :], s[0:16, tq, :], ALU.max)
            P.vspace(vector)
            mb = sb["m16"][0:16, None, :].broadcast_to([16, tmax, TT])
            vector.tensor_tensor(sb["esc"][0:16, 0:tmax, :], s[0:16, 0:tmax, :], mb, ALU.subtract)
            C["dv"].inc(P.vflush(vector))
        block.vector(c2v)
        dv_sub2 = C["dv"].v

        def c2a(scalar, tmax=tmax, dv_sub2=dv_sub2):
            P.wait(scalar, "scalar", C["dv"], dv_sub2)
            scalar.activation(sb["esc"][0:16, 0:tmax, :], sb["esc"][0:16, 0:tmax, :], AF.Exp)
            C["ac"].inc(P.aflush(scalar))
        block.scalar(c2a)
        ac_exp2 = C["ac"].v

        def c3v(vector, tmax=tmax, ac_exp2=ac_exp2):
            P.wait(vector, "vector", C["ac"], ac_exp2)
            e = sb["esc"]
            vector.tensor_tensor(sb["s16"][0:16, :], e[0:16, 0, :], e[0:16, 1, :], ALU.add)
            for tq in range(2, tmax):
                vector.tensor_tensor(sb["s16"][0:16, :], sb["s16"][0:16, :], e[0:16, tq, :], ALU.add)
            P.vspace(vector)
            vector.reciprocal(sb["r16"][0:16, :], sb["s16"][0:16, :])
            P.vspace(vector)
            rb = sb["r16"][0:16, None, :].broadcast_to([16, tmax, TT])
            vector.tensor_tensor(sb["en"][0:16, 0:tmax, :], e[0:16, 0:tmax, :], rb, ALU.mult)
            C["dv"].inc(P.vflush(vector))
        block.vector(c3v)
        dv_en = C["dv"].v

        for tq in range(tmax):
            r = min(tq, 15 - tq)
            hh = 0 if tq == r else 1

            def c4v(sync, si=si, n0=n0, r=r, hh=hh, tq=tq):
                if (t, tq) != (0, 0):
                    P.wait(sync, "sync", C["dv"], sb["vdup_guard"][0])
                for kap in range(4):
                    for dup in range(2):
                        srcr = io["gath"][r * 512 + 256 + kap * 64: r * 512 + 256 + (kap + 1) * 64,
                                          hh * SEQ + n0: hh * SEQ + n0 + TT]
                        C["lv"].inc(sync.dma_start(
                            out=sb["vdup"][dup * 64:(dup + 1) * 64, kap, :], in_=srcr), 16)
            block.sync(c4v)
            ld_v = C["lv"].v

            for ch in range(NCH):
                bi = 6 + ch % 2

                def c4r(tensor, ch=ch, bi=bi, tq=tq, dv_en=dv_en):
                    P.wait(tensor, "tensor", C["dv"], dv_en)
                    P.wait_bank(tensor, "tensor", bi)
                    C["pe"].inc(tensor.matmul(
                        pb[bi][:], sb["sel16"][:, ch * 128:(ch + 1) * 128],
                        sb["en"][0:16, tq, :], start=True, stop=True))
                block.tensor(c4r)
                pe_rep = C["pe"].v

                def c4m(vector, ch=ch, bi=bi, tq=tq, pe_rep=pe_rep, ld_v=ld_v):
                    P.wait(vector, "vector", C["pe"], pe_rep)
                    P.wait(vector, "vector", C["lv"], ld_v)
                    vsl = sb["vdup"][:, ch // 2, :]
                    if tq == 0:
                        inst = vector.tensor_tensor(
                            sb["attn_acc"][:, ch, :], pb[bi][:], vsl, ALU.mult)
                    else:
                        vector.tensor_tensor(sb["atmp"][ch % 2][:], pb[bi][:], vsl, ALU.mult)
                        inst = vector.tensor_tensor(
                            sb["attn_acc"][:, ch, :], sb["attn_acc"][:, ch, :],
                            sb["atmp"][ch % 2][:], ALU.add)
                    C["dv"].inc(inst)
                block.vector(c4m)
                P.set_bank_guard(bi, C["dv"], C["dv"].v)
            def c4fl(vector):
                C["dv"].inc(P.vflush(vector))
            block.vector(c4fl)
            sb["vdup_guard"][0] = C["dv"].v
        dv_attn = C["dv"].v

        def c5v(vector, dv_attn=dv_attn):
            vector.tensor_copy(sb["prod2"][:], sb["attn_acc"][:])
            C["dv"].inc(P.vflush(vector))
        block.vector(c5v)
        dv_attnbf = C["dv"].v

        for oc in range(NCH):
            gi = 3000 + t * 8 + oc
            wbuf = sb["wb2"][gi % 2]

            wsem = C["wq0"] if gi % 2 == 0 else C["wq1"]

            def c6g(g, gi=gi, oc=oc, wbuf=wbuf, wsem=wsem):
                if gi - 2 in wb_guard:
                    P.wait(g, "gpsimd", C["pe"], wb_guard[gi - 2])
                src_ = io["wproj_t"][:, oc * 128:(oc + 1) * 128].rearrange(
                    "(ic p) o -> p ic o", p=128)
                wsem.inc(g.dma_start(out=wbuf[:], in_=src_), 16)
            block.gpsimd(c6g)
            gw_done = wsem.v

            bi = 2 + gi % 2

            def c6t(tensor, wbuf=wbuf, bi=bi, gw_done=gw_done, dv_attnbf=dv_attnbf, wsem=wsem):
                P.wait(tensor, "tensor", wsem, gw_done)
                P.wait(tensor, "tensor", C["dv"], dv_attnbf)
                P.wait_bank(tensor, "tensor", bi)
                for ic in range(NCH):
                    inst = tensor.matmul(pb[bi][:], wbuf[:, ic, :], sb["prod2"][:, ic, :],
                                         start=(ic == 0), stop=(ic == NCH - 1))
                C["pe"].inc(inst)
            block.tensor(c6t)
            wb_guard[gi] = C["pe"].v
            pe_pr = C["pe"].v

            def c6v(vector, oc=oc, bi=bi, pe_pr=pe_pr):
                P.wait(vector, "vector", C["pe"], pe_pr)
                vector.tensor_tensor(sb["tmpf"][:, oc, :], sb["hidf"][:, oc, :],
                                     pb[bi][:], ALU.add)
                inst = vector.tensor_tensor(sb["hidf"][:, oc, :], sb["xb1"][:, oc, :],
                                            sb["tmpf"][:, oc, :], ALU.add)
                C["dv"].inc(inst)
            block.vector(c6v)
            P.set_bank_guard(bi, C["dv"], C["dv"].v)
        def c6fl(vector):
            C["dv"].inc(P.vflush(vector))
        block.vector(c6fl)
        dv_han = C["dv"].v

        def c7s(sync, t=t, dv_han=dv_han):
            P.wait(sync, "sync", C["dv"], dv_han)
            C["sp"].inc(sync.dma_start(out=io["accs"][t], in_=sb["hidf"][:]), 16)
            if _STAGE == "B":
                C["sp"].inc(sync.dma_start(out=io["out_flat"][t], in_=sb["hidf"][:]), 16)
        block.sync(c7s)

        def c7v(vector, dv_han=dv_han):
            vector.tensor_copy(sb["xb1"][:], sb["tmpf"][:])
            C["dv"].inc(P.vflush(vector))
        block.vector(c7v)
        dv_hancp = C["dv"].v

        dv_ln2 = _emit_ln_fm(P, sb, sb["xb1"][:],
                             affine=(sb["ln2w_pc"], sb["ln2b_pc"]), out_bf=sb["lnx_bf"],
                             dv_dep=dv_hancp)

        def c8s(sync, t=t, dv_ln2=dv_ln2):
            P.wait(sync, "sync", C["dv"], dv_ln2)
            C["sp"].inc(sync.dma_start(out=io["ln2s"][t], in_=sb["lnx_bf"][:]), 16)
        block.sync(c8s)

        def c9t(tensor, dv_ln2=dv_ln2):
            P.wait(tensor, "tensor", C["dv"], dv_ln2)
            P.wait_bank(tensor, "tensor", 4)
            for ic in range(NCH):
                inst = tensor.matmul(pb[4][0:8, :], sb["wrout_bf"][:, ic, :],
                                     sb["lnx_bf"][:, ic, :],
                                     start=(ic == 0), stop=(ic == NCH - 1))
            C["pe"].inc(inst)
        block.tensor(c9t)
        pe_lg = C["pe"].v

        def c9v(vector, pe_lg=pe_lg):
            P.wait(vector, "vector", C["pe"], pe_lg)
            vector.tensor_copy(sb["lg8"][0:8, :], pb[4][0:8, :])
            C["dv"].inc(P.vflush(vector))
        block.vector(c9v)
        dv_lg = C["dv"].v
        P.set_bank_guard(4, C["dv"], dv_lg)

        for tb in range(4):
            def c9tt2(tensor, tb=tb, dv_lg=dv_lg):
                P.wait(tensor, "tensor", C["dv"], dv_lg)
                P.wait_bank(tensor, "tensor", tb % 2)
                C["pe"].inc(tensor.transpose(
                    pb[tb % 2].ap()[0:128, 0:8],
                    sb["lg8"][0:8, tb * 128:(tb + 1) * 128], sb["ident_f32"][0:8, 0:8]))
            block.tensor(c9tt2)
            pe_lt = C["pe"].v

            def c9v2(vector, t=t, tb=tb, pe_lt=pe_lt):
                P.wait(vector, "vector", C["pe"], pe_lt)
                inst = vector.tensor_copy(sb["lgt"][:, tb, :], pb[tb % 2].ap()[0:128, 0:8])
                if _STAGE == "C1":
                    inst = vector.tensor_copy(sb["lgt_dbg"][:, t * 4 + tb, :],
                                              pb[tb % 2].ap()[0:128, 0:8])
                C["dv"].inc(P.vflush(vector) if tb == 3 else inst)
            block.vector(c9v2)
            P.set_bank_guard(tb % 2, C["dv"], C["dv"].v)
        dv_lgt = C["dv"].v

        for tb in range(4):
            def c10v(vector, tb=tb, dv_lgt=dv_lgt):
                P.wait(vector, "vector", C["dv"], dv_lgt)
                lg = sb["lgt"][:, tb, :]
                vector.max(sb["m8"][:], lg)
                P.vspace(vector)
                vector.tensor_scalar(sb["ind8"][:], lg, sb["m8"][:, 1:2], None, ALU.is_ge)
                vector.tensor_scalar(sb["negm"][:], sb["m8"][:, 0:1], -1.0, None, ALU.mult)
                C["dv"].inc(P.vflush(vector))
            block.vector(c10v)
            dv_t2 = C["dv"].v

            def c10a(scalar, tb=tb, dv_t2=dv_t2):
                P.wait(scalar, "scalar", C["dv"], dv_t2)
                scalar.activation(sb["e8"][:], sb["lgt"][:, tb, :], AF.Exp, bias=sb["negm"][:])
                scalar.activation(sb["e2b"][:], sb["m8"][:, 1:2], AF.Exp, bias=sb["negm"][:])
                C["ac"].inc(P.aflush(scalar))
            block.scalar(c10a)
            ac_t2 = C["ac"].v

            def c10v2(vector, t=t, tb=tb, ac_t2=ac_t2):
                P.wait(vector, "vector", C["ac"], ac_t2)
                vector.tensor_scalar(sb["e2b"][:], sb["e2b"][:], 1.0, None, ALU.add)
                P.vspace(vector)
                vector.reciprocal(sb["rden"][:], sb["e2b"][:])
                P.vspace(vector)
                vector.tensor_tensor(sb["w8"][:], sb["e8"][:], sb["ind8"][:], ALU.mult)
                P.vspace(vector)
                vector.tensor_scalar(sb["w8"][:], sb["w8"][:], sb["rden"][:], None, ALU.mult)
                C["dv"].inc(P.vflush(vector))
            block.vector(c10v2)
            dv_w8 = C["dv"].v

            def c10t(tensor, tb=tb, dv_w8=dv_w8):
                P.wait(tensor, "tensor", C["dv"], dv_w8)
                P.wait_bank(tensor, "tensor", tb % 2)
                C["pe"].inc(tensor.transpose(
                    pb[tb % 2].ap()[0:8, 0:128], sb["w8"][:], sb["ident_f32"][:]))
            block.tensor(c10t)
            pe_w = C["pe"].v

            def c10v3(vector, t=t, tb=tb, pe_w=pe_w):
                P.wait(vector, "vector", C["pe"], pe_w)
                inst = vector.tensor_copy(sb["w_em"][0:8, t * TT + tb * 128: t * TT + (tb + 1) * 128],
                                          pb[tb % 2].ap()[0:8, 0:128])
                C["dv"].inc(inst)
            block.vector(c10v3)
            P.set_bank_guard(tb % 2, C["dv"], C["dv"].v)

        def cfl(vector):
            C["dv"].inc(P.vflush(vector))
        block.vector(cfl)
        sb["b2_reuse_guard"][0] = C["dv"].v
        sb["sp_tile_guard"][0] = C["sp"].v



def _emit_phase_c(P, io, sb):
    block = P.block
    C = P.c
    pb = P.banks
    w1_guard = {}
    w2_guard = {}
    ln2_guard = {}

    def cgate_g(g):
        P.wait(g, "gpsimd", C["dv"], sb["phaseb_dv"][0])
        P.wait(g, "gpsimd", C["ac"], sb["phaseb_ac"][0])
        P.wait(g, "gpsimd", C["pe"], sb["phaseb_pe"][0])
        P.wait(g, "gpsimd", C["sp"], sb["phaseb_sp"][0])
    block.gpsimd(cgate_g)

    def cgate_s(sync):
        P.wait(sync, "sync", C["dv"], sb["phaseb_dv"][0])
        P.wait(sync, "sync", C["ac"], sb["phaseb_ac"][0])
        P.wait(sync, "sync", C["pe"], sb["phaseb_pe"][0])
    block.sync(cgate_s)

    for H in range(2):
        def d0(sync, H=H):
            if H == 0:
                P.wait(sync, "sync", C["sp"], sb["phaseb_sp"][0])
            else:
                P.wait(sync, "sync", C["so0"], C["so0"].v)
                P.wait(sync, "sync", C["so1"], C["so1"].v)
            for tt in range(4):
                C["ld"].inc(sync.dma_start(
                    out=sb["acc_h"][:, :, tt * TT:(tt + 1) * TT], in_=io["accs"][H * 4 + tt]), 16)
        block.sync(d0)
        ld_acc = C["ld"].v

        def d0v(vector, ld_acc=ld_acc):
            P.wait(vector, "vector", C["ld"], ld_acc)
        block.vector(d0v)

        for e in range(_NEXP):
            def d1g(g, H=H, e=e):
                key = (H, e)
                prev = (H, e - 1) if e > 0 else (H - 1, _NEXP - 1)
                if prev in w2_guard:
                    P.wait(g, "gpsimd", C["dv"], w2_guard[prev])
                src_ = io["moe_w2"][e].rearrange("(fq fp p) h -> p fq fp h", p=128, fq=8)
                for fq in range(8):
                    C["g2"].inc(g.dma_start(
                        out=sb["w2_sb"][:, fq * 4:(fq + 1) * 4, :], in_=src_[:, fq]), 16)
            block.gpsimd(d1g)
            gw_w2 = C["g2"].v

            for tt in range(4):
                bw = tt % 2

                def d2t(tensor, H=H, e=e, tt=tt, bw=bw):
                    P.wait(tensor, "tensor", C["dv"], sb["phaseb_dv"][0])
                    P.wait_bank(tensor, "tensor", bw)
                    C["pe"].inc(tensor.matmul(
                        pb[bw][:], sb["esel_sb"][0:8, e * 128:(e + 1) * 128],
                        sb["w_em"][0:8, (H * 4 + tt) * TT:(H * 4 + tt + 1) * TT],
                        start=True, stop=True))
                block.tensor(d2t)
                pe_wr = C["pe"].v

                def d2v(vector, tt=tt, bw=bw, pe_wr=pe_wr):
                    P.wait(vector, "vector", C["pe"], pe_wr)
                    inst = vector.tensor_copy(sb["wrep"][:, tt, :], pb[bw][:])
                    C["dv"].inc(P.vflush(vector) if tt == 3 else inst)
                block.vector(d2v)
                P.set_bank_guard(bw, C["dv"], C["dv"].v)
            dv_wr = C["dv"].v

            for tt in range(4):
                t_glob = H * 4 + tt
                def d3s(sync, t_glob=t_glob):
                    hist = sb["ln2_hist"]
                    if hist:
                        P.wait(sync, "sync", C["pe"], hist[-1])
                    C["ld"].inc(sync.dma_start(
                        out=sb["ln2t"][0][:], in_=io["ln2s"][t_glob]), 16)
                block.sync(d3s)
                ld_ln2 = C["ld"].v

                for fcB in range(8):
                    gi = H * 256 + e * 32 + tt * 8 + fcB
                    w1b = sb["w1_sb"][gi % 2]

                    wsem = C["w10"] if gi % 2 == 0 else C["w11"]

                    def d4g(g, H=H, e=e, fcB=fcB, gi=gi, w1b=w1b, wsem=wsem):
                        if gi - 2 in w1_guard:
                            P.wait(g, "gpsimd", C["pe"], w1_guard[gi - 2])
                        src_ = io["moe_w1"][e][:, fcB * 512:(fcB + 1) * 512].rearrange(
                            "(hc p) (fp f) -> p hc fp f", p=128, f=128)
                        wsem.inc(g.dma_start(out=w1b[:], in_=src_), 16)
                    block.gpsimd(d4g)
                    gw_w1 = wsem.v

                    for fp in range(4):
                        bi = 2 + (fcB * 4 + fp) % 2

                        def d5t(tensor, fp=fp, bi=bi, w1b=w1b, gw_w1=gw_w1, ld_ln2=ld_ln2, tt=tt, wsem=wsem):
                            P.wait(tensor, "tensor", wsem, gw_w1)
                            P.wait(tensor, "tensor", C["ld"], ld_ln2)
                            P.wait_bank(tensor, "tensor", bi)
                            for hc in range(NCH):
                                inst = tensor.matmul(pb[bi][:], w1b[:, hc, fp, :],
                                                     sb["ln2t"][0][:, hc, :],
                                                     start=(hc == 0), stop=(hc == NCH - 1))
                            C["pe"].inc(inst)
                        block.tensor(d5t)
                        pe_fc1 = C["pe"].v
                        if fp == 3:
                            w1_guard[gi] = pe_fc1
                            if fcB == 7:
                                sb["ln2_hist"].append(pe_fc1)

                        def d5a(scalar, fcB=fcB, fp=fp, bi=bi, pe_fc1=pe_fc1):
                            P.wait(scalar, "scalar", C["pe"], sb["prev_y_pe"][0])
                            P.wait(scalar, "scalar", C["pe"], pe_fc1)
                            inst = scalar.activation(sb["act_w"][:, fcB * 4 + fp, :],
                                                     pb[bi][:], AF.Gelu)
                            C["ac"].inc(P.aflush(scalar) if fp == 3 else inst)
                        block.scalar(d5a)
                        P.set_bank_guard(bi, C["ac"], C["ac"].v)
                ac_act = C["ac"].v

                for hc in range(NCH):
                    bi = 6 + hc % 2

                    def d7t(tensor, hc=hc, bi=bi, ac_act=ac_act, gw_w2=gw_w2):
                        P.wait(tensor, "tensor", C["g2"], gw_w2)
                        P.wait(tensor, "tensor", C["ac"], ac_act)
                        P.wait_bank(tensor, "tensor", bi)
                        for fc in range(32):
                            inst = tensor.matmul(pb[bi][:], sb["w2_sb"][:, fc, hc * 128:(hc + 1) * 128],
                                                 sb["act_w"][:, fc, :],
                                                 start=(fc == 0), stop=(fc == 31))
                        C["pe"].inc(inst)
                    block.tensor(d7t)
                    pe_y = C["pe"].v

                    def d7v(vector, H=H, e=e, hc=hc, bi=bi, tt=tt, pe_y=pe_y, dv_wr=dv_wr):
                        P.wait(vector, "vector", C["pe"], pe_y)
                        P.wait(vector, "vector", C["dv"], dv_wr)
                        if _STAGE == "C2" and H == 0 and e == 0 and tt == 0 and hc == 0:
                            vector.tensor_copy(sb["ydump"][:], pb[bi][:])
                        vector.tensor_tensor(sb["ctmp"][0][:], pb[bi][:],
                                             sb["wrep"][:, tt, :], ALU.mult)
                        inst = vector.tensor_tensor(
                            sb["acc_h"][:, hc, tt * TT:(tt + 1) * TT],
                            sb["acc_h"][:, hc, tt * TT:(tt + 1) * TT],
                            sb["ctmp"][0][:], ALU.add)
                        C["dv"].inc(inst)
                    block.vector(d7v)
                    P.set_bank_guard(bi, C["dv"], C["dv"].v)
                sb["prev_y_pe"][0] = C["pe"].v
                def d7fl(vector):
                    C["dv"].inc(P.vflush(vector))
                block.vector(d7fl)
                if _STAGE == "C2" and H == 0 and e == 0 and tt == 0:
                    def dmpc2(g):
                        P.wait(g, "gpsimd", C["dv"], C["dv"].v)
                        ofl = io["out"].rearrange("s n d -> (s n d)")
                        C["gw"].inc(g.dma_start(
                            out=ofl[0:128 * 32 * TT].rearrange("(p a n) -> p a n", p=128, a=32),
                            in_=sb["act_w"][:]), 16)
                        C["gw"].inc(g.dma_start(
                            out=ofl[128 * 32 * TT:128 * 32 * TT + 128 * TT]
                            .rearrange("(p n) -> p n", p=128), in_=sb["wrep"][:, 0, :]), 16)
                        C["gw"].inc(g.dma_start(
                            out=ofl[128 * 33 * TT:128 * 33 * TT + 128 * 8 * TT]
                            .rearrange("(p c n) -> p c n", p=128, c=8),
                            in_=sb["acc_h"][:, :, 0:TT]), 16)
                        C["gw"].inc(g.dma_start(
                            out=ofl[128 * 41 * TT:128 * 41 * TT + 128 * 4 * DM]
                            .rearrange("(p c n) -> p c n", p=128, c=4),
                            in_=sb["w2_sb"][:, 0:4, :]), 16)
                        C["gw"].inc(g.dma_start(
                            out=ofl[128 * 49 * TT:128 * 49 * TT + 128 * TT]
                            .rearrange("(p n) -> p n", p=128),
                            in_=sb["ydump"][:]), 16)
                    block.gpsimd(dmpc2)
            w2_guard[(H, e)] = C["dv"].v
        dv_half = C["dv"].v
        if _STAGE == "C2":
            continue

        for tt in range(4):
            for hc in range(NCH):
                for tb2 in range(4):
                    bj = tb2 % 2

                    def d8t(tensor, H=H, tt=tt, hc=hc, tb2=tb2, bj=bj, dv_half=dv_half):
                        P.wait(tensor, "tensor", C["dv"], dv_half)
                        P.wait_bank(tensor, "tensor", bj)
                        C["pe"].inc(tensor.transpose(
                            pb[bj].ap()[:, 0:128],
                            sb["acc_h"][:, hc, tt * TT + tb2 * 128: tt * TT + (tb2 + 1) * 128],
                            sb["ident_f32"][:]))
                    block.tensor(d8t)
                    pe_tr = C["pe"].v

                    def d8v(vector, bj=bj, pe_tr=pe_tr, og=sb["ost_guard"][bj]):
                        P.wait(vector, "vector", C["pe"], pe_tr)
                        if og:
                            P.wait(vector, "vector", og[0], og[1])
                        inst = vector.tensor_copy(sb["ost"][bj][:], pb[bj].ap()[:, 0:128])
                        C["dv"].inc(P.vflush(vector))
                    block.vector(d8v)
                    dv_o = C["dv"].v
                    P.set_bank_guard(bj, C["dv"], dv_o)

                    osem = C["so0"] if bj == 0 else C["so1"]

                    def d8s(sync, H=H, tt=tt, hc=hc, tb2=tb2, bj=bj, dv_o=dv_o, osem=osem):
                        P.wait(sync, "sync", C["dv"], dv_o)
                        dst = io["out"][H, tt * TT + tb2 * 128: tt * TT + (tb2 + 1) * 128,
                                        hc * 128:(hc + 1) * 128]
                        osem.inc(sync.dma_start(out=dst, in_=sb["ost"][bj][:]), 16)
                    block.sync(d8s)
                    sb["ost_guard"][bj] = (osem, osem.v)
        sb["out_guard"][0] = C["sp"].v
        if True:
            pass



def build_program():
    nc = bass.Bass("TRN2", target_bir_lowering=False)

    io = {}
    for nm, shp, dt in [
        ("xs", [2, SEQ, DM], F32),
        ("wqkv_t", [DM, 3072], F32), ("bqkv_pc", [128, 24], F32),
        ("wgate_t", [DM, DM], F32), ("bgate_pc", [128, 8], F32),
        ("wout_t", [DM, DM], F32), ("bout_pc", [128, 8], F32),
        ("wq22", [128, 22], F32), ("prior_row", [1, 176], F32),
        ("bypass16", [16, 1], F32),
        ("ln1w_pc", [128, 8], F32), ("ln1b_pc", [128, 8], F32),
        ("ln2w_pc", [128, 8], F32), ("ln2b_pc", [128, 8], F32),
        ("wqkv2_t", [DM, 1536], F32), ("wproj_t", [DM, DM], F32),
        ("wrout", [DM, 8], F32), ("wrout_pc", [128, 8, 8], F32),
        ("moe_w1", [8, DM, 4096], F32), ("moe_w2", [8, 4096, DM], F32),
        ("ident_bf", [128, 128], BF16), ("ident_f32", [128, 128], F32),
        ("ones1_f32", [1, 128], F32),
        ("sel16", [16, 8 * 128], BF16), ("indT", [128, 8 * 16], BF16),
        ("esel", [8, 8 * 128], BF16), ("mbig", [640, 11 * 128], BF16),
        ("mask_pc", [16, 32], F32),
    ]:
        io[nm] = nc.dram_tensor(nm, shp, dt, kind="ExternalInput").ap()

    out = nc.dram_tensor("out", [2, SEQ, DM], F32, kind="ExternalOutput").ap()
    io["out"] = out
    io["out_flat"] = out.rearrange("s n d -> (s n d)").rearrange(
        "(t p c n) -> t p c n", t=8, p=128, c=NCH)

    io["x1s"] = nc.dram_tensor("x1s", [NTILE, 128, NCH, TT], F32).ap()
    io["hidden_s"] = nc.dram_tensor("hidden_s", [NTILE, 128, NCH, TT], F32).ap()
    io["q2s"] = nc.dram_tensor("q2s", [NTILE, 128, NCH, TT], BF16).ap()
    io["ln2s"] = nc.dram_tensor("ln2s", [NTILE, 128, NCH, TT], BF16).ap()
    io["accs"] = nc.dram_tensor("accs", [NTILE, 128, NCH, TT], F32).ap()
    io["kvc"] = nc.dram_tensor("kvc", [512, 4096], BF16).ap()
    io["gath"] = nc.dram_tensor("gath", [4096, 4096], BF16, addr_space="Shared").ap()

    with ExitStack() as ctx:
        E = ctx.enter_context
        C = {}
        for nm in ["ld", "gw", "pe", "dv", "ac", "pl", "sp", "cc", "g2", "lk", "lv", "gs",
                   "wq0", "wq1", "w10", "w11", "sx0", "sx1", "so0", "so1"]:
            C[nm] = Ctr(E(nc.semaphore(nm)), nm)

        pb = [E(nc.psum_tensor(f"pb{i}", [128, TT], F32)) for i in range(8)]

        sb = {}
        for nm, shp, dt in [
            ("ident_bf", [128, 128], BF16), ("ident_f32", [128, 128], F32),
            ("ones1_f32", [1, 128], F32),
            ("ones128c_f32", [128, 1], F32), ("ones128c_bf", [128, 1], BF16),
            ("sel16", [16, 8 * 128], BF16), ("indT", [128, 8 * 16], BF16),
            ("esel_sb", [8, 8 * 128], BF16),
            ("wq22_bf", [128, 22], BF16),
            ("prior_row_sb", [1, 176], F32), ("prior_bc", [128, 176], F32),
            ("bqkv_pc", [128, 24], F32), ("bgate_pc", [128, 8], F32),
            ("bout_pc", [128, 8], F32),
            ("ln1w_pc", [128, 8], F32), ("ln1b_pc", [128, 8], F32),
            ("ln2w_pc", [128, 8], F32), ("ln2b_pc", [128, 8], F32),
            ("mask_sb", [16, 32], F32),
            ("wrout_bf", [128, 8, 8], BF16),
            ("c0_sp", [16, 1], F32), ("byp", [16, 1], F32), ("byp2", [16, 1], F32),
            ("fl_v", [1, 128], F32), ("fl_a", [1, 128], F32),
            ("w_em", [8, NTILE * TT], BF16),
        ]:
            sb[nm] = E(nc.sbuf_tensor("s_" + nm, shp, dt))

        block = E(nc.Block())
        P = Prog(nc, block, C, pb)
        P.fl_v = sb["fl_v"]
        P.fl_a = sb["fl_a"]

        def setup_sync(sync):
            for nm in ["ident_bf", "ident_f32", "ones1_f32", "sel16", "indT",
                       "bqkv_pc", "bgate_pc", "bout_pc", "prior_row_sb",
                       "ln1w_pc", "ln1b_pc", "ln2w_pc", "ln2b_pc", "mask_sb",
                       "esel_sb"]:
                key = {"prior_row_sb": "prior_row", "esel_sb": "esel",
                       "mask_sb": "mask_pc"}.get(nm, nm)
                C["ld"].inc(sync.dma_start(out=sb[nm][:], in_=io[key]), 16)
            C["ld"].inc(sync.dma_start(out=sb["byp"][:], in_=io["bypass16"]), 16)
            sync.wait_ge(C["ld"].sem, C["ld"].v)
        block.sync(setup_sync)
        ld_const = C["ld"].v

        def setup_gp(g):
            C["gs"].inc(g.dma_start(out=sb["wq22_bf"][:], in_=io["wq22"]), 16)
            C["gs"].inc(g.dma_start(
                out=sb["ones128c_bf"][:],
                in_=io["ones1_f32"].rearrange("a p -> p a")), 16)
            C["gs"].inc(g.dma_start(
                out=sb["ones128c_f32"][:],
                in_=io["ones1_f32"].rearrange("a p -> p a")), 16)
            C["gs"].inc(g.dma_start(out=sb["wrout_bf"][:], in_=io["wrout_pc"]), 16)
        block.gpsimd(setup_gp)
        gw_const = C["gs"].v

        def setup_pe(tensor):
            P.wait(tensor, "tensor", C["ld"], ld_const)
            C["pe"].inc(tensor.matmul(pb[0][:, 0:176], sb["ones1_f32"][:],
                                      sb["prior_row_sb"][:], start=True, stop=True))
        block.tensor(setup_pe)
        pe_prior = C["pe"].v

        def setup_v(vector):
            P.wait(vector, "vector", C["pe"], pe_prior)
            C["dv"].inc(vector.tensor_copy(sb["prior_bc"][:], pb[0][:, 0:176]))
        block.vector(setup_v)
        P.set_bank_guard(0, C["dv"], C["dv"].v)

        def setup_sp1(scalar):
            P.wait(scalar, "scalar", C["ld"], ld_const)
            scalar.activation(sb["byp2"][:], sb["byp"][:], AF.Abs)
            scalar.drain()
            scalar.activation(sb["byp2"][:], sb["byp2"][:], AF.Exp, scale=-1.0)
            scalar.drain()
            scalar.activation(sb["byp2"][:], sb["byp2"][:], AF.Ln, bias=1.0)
            C["ac"].inc(P.aflush(scalar))
        block.scalar(setup_sp1)
        ac_sp = C["ac"].v

        def setup_sp2(vector):
            P.wait(vector, "vector", C["ac"], ac_sp)
            vector.tensor_scalar(sb["c0_sp"][:], sb["byp"][:], 0.0, None, ALU.max)
            P.vspace(vector)
            vector.tensor_tensor(sb["c0_sp"][:], sb["c0_sp"][:], sb["byp2"][:], ALU.add)
            P.vspace(vector)
            vector.tensor_scalar(sb["c0_sp"][:], sb["c0_sp"][:], float(D4[0]), None, ALU.add)
            C["dv"].inc(P.vflush(vector))
        block.vector(setup_sp2)
        dv_setup = C["dv"].v

        def gate_consts(tensor):
            P.wait(tensor, "tensor", C["dv"], dv_setup)
            P.wait(tensor, "tensor", C["ld"], ld_const)
        block.tensor(gate_consts)

        with ExitStack() as actx:
            A = actx.enter_context
            sb["x_tm"] = [A(nc.sbuf_tensor("x_tm0", [128, 4, DM], F32))]
            sb["xn_tm"] = A(nc.sbuf_tensor("xn_tm", [128, 4, DM], BF16))
            sb["xn_fm"] = A(nc.sbuf_tensor("xn_fm", [128, NCH, TT], BF16))
            sb["q_fm"] = A(nc.sbuf_tensor("q_fm", [128, NCH, TT], BF16))
            sb["k_buf"] = A(nc.sbuf_tensor("k_buf", [128, NCH, SEQ], BF16))
            sb["v_buf"] = A(nc.sbuf_tensor("v_buf", [128, NCH, SEQ], BF16))
            sb["prod"] = A(nc.sbuf_tensor("prod", [128, NCH, TT], BF16))
            sb["acc"] = A(nc.sbuf_tensor("acc", [128, NCH, TT], F32))
            sb["gate_sb"] = A(nc.sbuf_tensor("gate_sb", [128, NCH, TT], BF16))
            sb["mtmp"] = [A(nc.sbuf_tensor(f"mtmp{i}", [128, TT], BF16)) for i in range(2)]
            sb["w16_sb"] = A(nc.sbuf_tensor("w16_sb", [16, TT], BF16))
            sb["y_sb"] = A(nc.sbuf_tensor("y_sb", [128, TT], F32))
            sb["x1st"] = [A(nc.sbuf_tensor(f"x1st{i}", [128, TT], F32)) for i in range(2)]
            sb["wqkv_oc"] = [A(nc.sbuf_tensor(f"wqkv_oc{i}", [128, NCH, 128], BF16))
                             for i in range(2)]
            sb["lnmean"] = A(nc.sbuf_tensor("lnmean", [128, 4], F32))
            sb["lnmv"] = A(nc.sbuf_tensor("lnmv", [128, 4, 2], F32))
            sb["lnvar"] = A(nc.sbuf_tensor("lnvar", [128, 4], F32))
            sb["lnsd"] = A(nc.sbuf_tensor("lnsd", [128, 4], F32))
            sb["lnrstd"] = A(nc.sbuf_tensor("lnrstd", [128, 4], F32))
            sb["qo_sb"] = A(nc.sbuf_tensor("qo_sb", [128, 176], F32))
            sb["qo_e"] = A(nc.sbuf_tensor("qo_e", [128, 176], F32))
            sb["qo_x"] = A(nc.sbuf_tensor("qo_x", [128, 176], F32))
            sb["qo_m"] = A(nc.sbuf_tensor("qo_m", [128, 16], F32))
            sb["qo_s"] = A(nc.sbuf_tensor("qo_s", [128, 16], F32))
            sb["qo_r"] = A(nc.sbuf_tensor("qo_r", [128, 16], F32))
            sb["qo_rv"] = A(nc.sbuf_tensor("qo_rv", [128, 3, 16], F32))
            sb["g123_tm"] = A(nc.sbuf_tensor("g123_tm", [128, 4, 528], BF16))
            sb["G_hm"] = A(nc.sbuf_tensor("G_hm", [128, 5, TT], BF16))
            sb["mbig_sb"] = A(nc.sbuf_tensor("mbig_sb", [128, 5, 11 * 128], BF16))
            sb["coefS"] = A(nc.sbuf_tensor("coefS", [128, 11, TT], BF16))

            def setup_gp2(g):
                C["gs"].inc(g.dma_start(
                    out=sb["mbig_sb"][:],
                    in_=io["mbig"].rearrange("(rb p) c -> p rb c", p=128)), 16)
            block.gpsimd(setup_gp2)
            gw_mbig = C["gs"].v

            def gate_mbig(tensor):
                P.wait(tensor, "tensor", C["gs"], gw_mbig)
            block.tensor(gate_mbig)

            _emit_phase_a(P, io, sb)

            def a_end(vector):
                P.wait(vector, "vector", C["sp"], C["sp"].v)
                P.wait(vector, "vector", C["sx0"], C["sx0"].v)
                P.wait(vector, "vector", C["sx1"], C["sx1"].v)
                C["dv"].inc(P.vflush(vector))
            block.vector(a_end)

            def a_end_a(scalar):
                C["ac"].inc(P.aflush(scalar))
            block.scalar(a_end_a)

            def a_end_t(tensor):
                C["pe"].inc(tensor.drain())
            block.tensor(a_end_t)
            sb["phasea_dv"] = C["dv"].v
            sb["phasea_ac"] = C["ac"].v
            sb["phasea_pe"] = C["pe"].v

        if _STAGE in ("full", "B", "C1", "C2"):
            with ExitStack() as bctx:
                B = bctx.enter_context
                sb["xb1"] = B(nc.sbuf_tensor("xb1", [128, NCH, TT], F32))
                sb["hidf"] = B(nc.sbuf_tensor("hidf", [128, NCH, TT], F32))
                sb["tmpf"] = B(nc.sbuf_tensor("tmpf", [128, NCH, TT], F32))
                sb["sqscr"] = B(nc.sbuf_tensor("sqscr", [128, NCH, TT], BF16))
                sb["lnx_bf"] = B(nc.sbuf_tensor("lnx_bf", [128, NCH, TT], BF16))
                sb["q2t"] = B(nc.sbuf_tensor("q2t", [128, NCH, TT], BF16))
                sb["kvstage"] = B(nc.sbuf_tensor("kvstage", [128, 4, TT], BF16))
                sb["wb2"] = [B(nc.sbuf_tensor(f"wb2{i}", [128, NCH, 128], BF16))
                             for i in range(2)]
                sb["rowM"] = B(nc.sbuf_tensor("rowM", [1, TT], F32))
                sb["rowT"] = B(nc.sbuf_tensor("rowT", [1, TT], F32))
                sb["rowV"] = B(nc.sbuf_tensor("rowV", [1, TT], F32))
                sb["rowSD"] = B(nc.sbuf_tensor("rowSD", [1, TT], F32))
                sb["rowA"] = B(nc.sbuf_tensor("rowA", [1, TT], F32))
                sb["rowB"] = B(nc.sbuf_tensor("rowB", [1, TT], F32))
                sb["A_s"] = B(nc.sbuf_tensor("A_s", [128, TT], F32))
                sb["B_s"] = B(nc.sbuf_tensor("B_s", [128, TT], F32))
                sb["scores"] = B(nc.sbuf_tensor("scores", [16, 16, TT], BF16))
                sb["esc"] = B(nc.sbuf_tensor("esc", [16, 16, TT], BF16))
                sb["en"] = B(nc.sbuf_tensor("en", [16, 16, TT], BF16))
                sb["m16"] = B(nc.sbuf_tensor("m16", [16, TT], F32))
                sb["s16"] = B(nc.sbuf_tensor("s16", [16, TT], F32))
                sb["r16"] = B(nc.sbuf_tensor("r16", [16, TT], F32))
                sb["kdup"] = B(nc.sbuf_tensor("kdup", [128, 4, TT], BF16))
                sb["vdup"] = B(nc.sbuf_tensor("vdup", [128, 4, TT], BF16))
                sb["prod2"] = B(nc.sbuf_tensor("prod2", [128, NCH, TT], BF16))
                sb["attn_acc"] = B(nc.sbuf_tensor("attn_acc", [128, NCH, TT], F32))
                sb["atmp"] = [B(nc.sbuf_tensor(f"atmp{i}", [128, TT], BF16)) for i in range(2)]
                sb["lg8"] = B(nc.sbuf_tensor("lg8", [8, TT], F32))
                sb["lgt"] = B(nc.sbuf_tensor("lgt", [128, 4, 8], F32))
                sb["m8"] = B(nc.sbuf_tensor("m8", [128, 8], F32))
                sb["ind8"] = B(nc.sbuf_tensor("ind8", [128, 8], F32))
                sb["negm"] = B(nc.sbuf_tensor("negm", [128, 1], F32))
                sb["e8"] = B(nc.sbuf_tensor("e8", [128, 8], F32))
                sb["e2b"] = B(nc.sbuf_tensor("e2b", [128, 1], F32))
                sb["rden"] = B(nc.sbuf_tensor("rden", [128, 1], F32))
                sb["w8"] = B(nc.sbuf_tensor("w8", [128, 8], F32))
                sb["lgt_dbg"] = B(nc.sbuf_tensor("lgt_dbg", [128, NTILE * 4, 8], F32))
                sb["b2_reuse_guard"] = [0]
                sb["kdup_guard"] = [0]
                sb["vdup_guard"] = [0]
                sb["sp_tile_guard"] = [C["sp"].v]

                _emit_phase_b(P, io, sb)

                if _STAGE == "C1":
                    def bdump(g):
                        P.wait(g, "gpsimd", C["sp"], C["sp"].v)
                        for t in range(NTILE):
                            C["gw"].inc(g.dma_start(out=io["out_flat"][t],
                                                    in_=io["ln2s"][t]), 16)
                    block.gpsimd(bdump)

                    def bdump2(g):
                        P.wait(g, "gpsimd", C["dv"], sb["b2_reuse_guard"][0])
                        C["gw"].inc(g.dma_start(
                            out=io["out"].rearrange("s n d -> (s n d)")
                            .rearrange("(z r c) -> z r c", z=2, r=8)[1, :, 0:NTILE * TT],
                            in_=sb["w_em"][:]), 16)
                        C["gw"].inc(g.dma_start(
                            out=io["out"].rearrange("s n d -> (s n d)")
                            .rearrange("(z r c) -> z r c", z=2, r=64)[1, 32:64, 0:NTILE * 4 * 8],
                            in_=sb["lgt_dbg"][0:32, :, :].rearrange("p a e -> p (a e)")), 16)
                    block.gpsimd(bdump2)

                def b_end(vector):
                    P.wait(vector, "vector", C["sp"], C["sp"].v)
                    P.wait(vector, "vector", C["gw"], C["gw"].v)
                    C["dv"].inc(P.vflush(vector))
                block.vector(b_end)

                def b_end_a(scalar):
                    C["ac"].inc(P.aflush(scalar))
                block.scalar(b_end_a)

                def b_end_t(tensor):
                    C["pe"].inc(tensor.drain())
                block.tensor(b_end_t)

            with ExitStack() as cctx:
              if _STAGE in ("full", "C2"):
                  Cx = cctx.enter_context
                  sb["acc_h"] = Cx(nc.sbuf_tensor("acc_h", [128, NCH, 4 * TT], F32))
                  sb["w2_sb"] = Cx(nc.sbuf_tensor("w2_sb", [128, 32, DM], BF16))
                  sb["w1_sb"] = [Cx(nc.sbuf_tensor(f"w1_sb{i}", [128, NCH, 4, 128], BF16))
                                 for i in range(2)]
                  sb["ln2t"] = [Cx(nc.sbuf_tensor("ln2t0", [128, NCH, TT], BF16))]
                  sb["act_w"] = Cx(nc.sbuf_tensor("act_w", [128, 32, TT], BF16))
                  sb["wrep"] = Cx(nc.sbuf_tensor("wrep", [128, 4, TT], BF16))
                  sb["ctmp"] = [Cx(nc.sbuf_tensor("ctmp0", [128, TT], BF16))]
                  sb["ost"] = [Cx(nc.sbuf_tensor(f"ost{i}", [128, 128], F32)) for i in range(2)]
                  if _STAGE == "C2":
                      sb["ydump"] = Cx(nc.sbuf_tensor("ydump", [128, TT], BF16))
                  sb["ln2_hist"] = []
                  sb["prev_y_pe"] = [0]
                  sb["phaseb_ac"] = [C["ac"].v]
                  sb["phaseb_pe"] = [C["pe"].v]
                  sb["phaseb_dv"] = [C["dv"].v]
                  sb["out_guard"] = [0]
                  sb["ost_guard"] = [None, None]
                  sb["phaseb_sp"] = [C["sp"].v]

                  _emit_phase_c(P, io, sb)

        def fin(g):
            for nm in ["sp", "gw", "sx0", "sx1", "so0", "so1", "gs", "wq0", "wq1",
                       "w10", "w11", "g2", "lk", "lv", "ld"]:
                P.wait(g, "gpsimd", C[nm], C[nm].v)
        block.gpsimd(fin)

    return nc


def kernel(**inputs):
    in_maps = _prep_host(inputs)
    nc = build_program()
    res = run_bass_kernel_spmd(nc, in_maps, core_ids=list(range(NCORE)))
    out = np.empty((16, SEQ, DM), np.float32)
    for cid in range(NCORE):
        o = res.results[cid]["out"]
        out[cid] = o[0]
        out[15 - cid] = o[1]
    return out
